# version 05
# baseline (speedup 1.0000x reference)
import numpy as np
import concourse.bass as bass
import concourse.mybir as mybir
from concourse.bass_utils import run_bass_kernel_spmd
from contextlib import ExitStack

F32 = mybir.dt.float32
BF16 = mybir.dt.bfloat16
I32 = mybir.dt.int32
AF = mybir.ActivationFunctionType
ALU = mybir.AluOpType
AX = mybir.AxisListType

NC = 8
TPC = 2048
NCH = 16
D = 1024
DB = 8
SHIFT = 16.0
EPS = 1e-5
STAGE = 4
DEBUG = False
DEDUPE = True

ENGINES = ("sync", "vector", "scalar", "tensor", "gpsimd")


class Sem:
    def __init__(self, nc, ctx, name):
        self.h = ctx.enter_context(nc.semaphore(name))
        self.n = 0


class Prog:

    def __init__(self, sems):
        self.items = []
        self.own = sems
        self.last_wait = {}

    def E(self, eng, meth, *args, inc=None, inck=1, **kw):
        own = self.own.get(eng)
        is_dma = meth.startswith("dma_start") or meth == "collective_compute"
        if own is not None and not is_dma:
            self.W(eng, own)
            if inc is None:
                inc, inck = own, 1
        if inc is not None:
            inc.n += inck
        def thunk(e, meth=meth, args=args, kw=kw, inc=inc, inck=inck):
            ins = getattr(e, meth)(*args, **kw)
            if inc is not None:
                ins.then_inc(inc.h, inck)
        self.items.append((eng, thunk))

    def W(self, eng, sem, val=None):
        v = sem.n if val is None else val
        key = (eng, id(sem))
        if DEDUPE and self.last_wait.get(key, -1) >= v:
            return
        self.last_wait[key] = v
        def thunk(e, sem=sem, v=v):
            e.wait_ge(sem.h, v)
        self.items.append((eng, thunk))

    def CC(self, kind, op, ins_, outs_, inc):
        inc.n += 1
        def thunk(e, kind=kind, op=op, ins_=ins_, outs_=outs_, inc=inc):
            e.collective_compute(kind, op, replica_groups=[list(range(NC))],
                                 ins=ins_, outs=outs_).then_inc(inc.h, 1)
        self.items.append(("gpsimd", thunk))

    def bar(self):
        for e in ("vector", "scalar", "tensor", "gpsimd"):
            for s in self.own.values():
                self.W(e, s)


def build():
    nc = bass.Bass("TRN2", target_bir_lowering=False)

    def din(name, shape, dtype=F32):
        return nc.dram_tensor(name, shape, dtype, kind="ExternalInput").ap()

    x_in = din("x_sh", [TPC, D])
    wq0 = din("wq0", [64, D]); wk0 = din("wk0", [64, D]); wv0 = din("wv0", [64, D])
    wfc_sl = din("wfc_sl", [512, D])
    bfc_in = din("bfc", [32, 128])
    afc_in = din("afc", [16, D])
    bfcL_in = din("bfcL", [4096, 16])
    wpr_sl = din("wpr_sl", [D, 512])
    bpr_in = din("bpr", [1, D])
    apr_in = din("apr", [16, 4096])
    bprL_in = din("bprL", [D, 16])
    q1 = din("q1w", [D, D]); k1 = din("k1w", [D, D]); v1 = din("v1w", [D, D])
    o1 = din("o1w", [D, D])
    gw_in = din("gw", [D, 80])
    gb_in = din("gbias", [2, 16])
    phi_in = din("phi", [D, 2048])
    wexp_in = din("wexp", [8, D, D])
    bexp_in = din("bexp", [64, D])
    mask_in = din("mask", [1, 8])
    y_out = nc.dram_tensor("y", [TPC, D], F32, kind="ExternalOutput").ap()
    dbg_sc = nc.dram_tensor("dbg_sc", [128, 192], F32, kind="ExternalOutput").ap()
    dbg_a = nc.dram_tensor("dbg_a", [128, 512], F32, kind="ExternalOutput").ap()
    dbg_b = nc.dram_tensor("dbg_b", [128, 512], F32, kind="ExternalOutput").ap()
    dbg_c = nc.dram_tensor("dbg_c", [128, 512], F32, kind="ExternalOutput").ap()
    dbg_d = nc.dram_tensor("dbg_d", [128, 512], F32, kind="ExternalOutput").ap()
    dbg_e = nc.dram_tensor("dbg_e", [128, 512], F32, kind="ExternalOutput").ap()
    dbg_f = nc.dram_tensor("dbg_f", [128, 512], F32, kind="ExternalOutput").ap()
    dbg_g = nc.dram_tensor("dbg_g", [128, 512], BF16, kind="ExternalOutput").ap()
    dbg_h = nc.dram_tensor("dbg_h", [64, 2048], F32, kind="ExternalOutput").ap()
    dbg_i = nc.dram_tensor("dbg_i", [128, 128], BF16, kind="ExternalOutput").ap()
    dbg_j = nc.dram_tensor("dbg_j", [128, 1024], F32, kind="ExternalOutput").ap()
    dbg_k = nc.dram_tensor("dbg_k", [128, 1024], F32, kind="ExternalOutput").ap()
    dbg_l = nc.dram_tensor("dbg_l", [128, 1024], F32, kind="ExternalOutput").ap()
    dbg_m = nc.dram_tensor("dbg_m", [128, 512], BF16, kind="ExternalOutput").ap()

    d_mx_i = nc.dram_tensor("d_mx_i", [1, 4], F32, kind="Internal")
    d_mx_o = nc.dram_tensor("d_mx_o", [1, 4], F32, kind="Internal")
    d_q_i = nc.dram_tensor("d_q_i", [64, TPC], BF16, kind="Internal")
    d_q_o = nc.dram_tensor("d_q_o", [NC, 64, TPC], BF16, kind="Internal")
    d_rf_i = nc.dram_tensor("d_rf_i", [D, 512], BF16, kind="Internal")
    d_rf_o = nc.dram_tensor("d_rf_o", [NC, D, 512], BF16, kind="Internal")
    d_rp_i = nc.dram_tensor("d_rp_i", [512, D], BF16, kind="Internal")
    d_rp_o = nc.dram_tensor("d_rp_o", [NC, 512, D], BF16, kind="Internal")
    d_o0_i = nc.dram_tensor("d_o0_i", [NC * TPC, 64], F32, kind="Internal")
    d_o0_o = nc.dram_tensor("d_o0_o", [TPC, 64], F32, kind="Internal")
    d_st_i = nc.dram_tensor("d_st_i", [128, 1056], F32, kind="Internal")
    d_st_o = nc.dram_tensor("d_st_o", [NC, 128, 1056], F32, kind="Internal")
    d_S_i = nc.dram_tensor("d_S_i", [128, 8], F32, kind="Internal")
    d_S_o = nc.dram_tensor("d_S_o", [128, 8], F32, kind="Internal")
    d_yn_i = nc.dram_tensor("d_yn_i", [2, 4096], F32, kind="Internal")
    d_yn_o = nc.dram_tensor("d_yn_o", [NC, 2, 4096], F32, kind="Internal")

    with ExitStack() as ctx:
        sb = lambda n_, s_, d_=F32: ctx.enter_context(nc.sbuf_tensor(n_, s_, d_))
        pst = lambda n_, s_, d_=F32: ctx.enter_context(nc.psum_tensor(n_, s_, d_))

        xres = sb("xres", [128, NCH, D])
        A = sb("A", [128, 16384], BF16)
        B = sb("B", [128, 16384], BF16)
        C = sb("C", [128, 16384], BF16)
        SF = sb("SF", [128, 2048])
        ST = sb("ST", [64, 2048])
        WS = sb("WS", [128, 4096], BF16)
        WF = sb("WF", [128, 1024])
        tmb = sb("tmb", [128, D], BF16)
        VQ = sb("VQ", [128, 1024], BF16)
        stateF = sb("stateF", [128, 1024])
        s2st = sb("s2st", [128, 256]); afg = sb("afg", [128, 256])
        asg = sb("asg", [128, 256])
        SC = sb("SC", [128, 192])
        idb = sb("idb", [128, 128], BF16)
        idf = sb("idf", [128, 128])
        epsv = sb("epsv", [128, 1])
        onesf = sb("onesf", [128, 1])
        ones1 = sb("ones1", [1, 128])
        bn6 = sb("bn6", [128, 2, 6])
        mv = sb("mv", [128, 2])
        sdt = sb("sdt", [128, 1]); rstd = sb("rstd", [128, 1]); nmr = sb("nmr", [128, 1])
        evf0 = sb("evf0", [128, 64]); evf1 = sb("evf1", [128, 64])
        vnb0 = sb("vnb0", [128, 64], BF16); vnb1 = sb("vnb1", [128, 64], BF16)
        rts0 = sb("rts0", [128, 128], BF16); rts1 = sb("rts1", [128, 128], BF16)
        afcT = sb("afcT", [128, 128], BF16)
        bfct0f = sb("bfct0", [16, 128], BF16); bfct1f = sb("bfct1", [16, 128], BF16)
        bfct0 = bfct0f[:]; bfct1 = bfct1f[:]
        bfcv = sb("bfcv", [128, 32])
        gwb = sb("gwb", [128, 640], BF16)
        gbias = sb("gbias_bc", [128, 80])
        wprod = sb("wprod", [1, 32])
        maskb = sb("maskb", [128, 8])
        vsl = sb("vsl", [128, 64], BF16)
        Sfm = sb("Sfm", [128, 8]); Sbf = sb("Sbf", [128, 8], BF16)

        P0 = pst("P0", [128, 1024]); P1 = pst("P1", [128, 1024])
        P2 = pst("P2", [128, 1024]); P3 = pst("P3", [128, 1024])
        PB2 = P2[:].bitcast(BF16)
        PB3 = P3[:].bitcast(BF16)

        sV = Sem(nc, ctx, "sV"); sS = Sem(nc, ctx, "sS")
        sT = Sem(nc, ctx, "sT"); sG = Sem(nc, ctx, "sG")
        sLD = Sem(nc, ctx, "sLD"); sGD = Sem(nc, ctx, "sGD")
        sVD = Sem(nc, ctx, "sVD"); sSTR = Sem(nc, ctx, "sSTR")
        sCC = Sem(nc, ctx, "sCC")

        P = Prog({"vector": sV, "scalar": sS, "tensor": sT, "gpsimd": sG})
        E, W, CC = P.E, P.W, P.CC

        Af = A[:].bitcast(F32)
        Bf = B[:].bitcast(F32)
        Cf = C[:].bitcast(F32)
        A3 = A[:].rearrange("p (a b) -> p a b", a=DB)
        C3h = C[:].rearrange("p (a b) -> p a b", a=4)
        w0T = {"q": WS[:, 0:512], "k": WS[:, 512:1024], "v": WS[:, 1024:1536]}
        rbq = WS[:, 1536:2560]
        wm = ST[0:1, 1536:2048]
        SFB = SF[:].bitcast(BF16)
        kfm = SFB[0:64, 2048:4096]
        vtm3 = tmb[:].rearrange("p (a b) -> p a b", a=NCH)
        aprT = WS[:, 3072:3584]
        bfcb = WS[:, 3584:4096]
        chT0 = VQ[0:64, 0:128]; chT1 = VQ[0:64, 128:256]
        gwb3 = gwb[:].rearrange("p (a g) -> p a g", a=DB)
        st4 = stateF[:].rearrange("p (a r e) -> p a r e", a=8, r=2)
        C3v = C[:].rearrange("p (c h e) -> p c h e", c=NCH, h=16)
        B3v = B[:].rearrange("p (c h e) -> p c h e", c=NCH, h=16)

        tmb_guard = [0]

        def ln_common(src_ap):
            E("vector", "bn_stats", bn6[:, 0, :], src_ap[:, 0:512])
            E("vector", "bn_stats", bn6[:, 1, :], src_ap[:, 512:1024])
            E("vector", "bn_aggr", mv[:], bn6[:])
            W("scalar", sV)
            E("scalar", "activation", sdt[:], mv[:, 1:2], AF.Sqrt, bias=epsv[:])
            W("vector", sS)
            E("vector", "reciprocal", rstd[:], sdt[:])
            E("vector", "tensor_scalar", nmr[:], mv[:, 0:1], rstd[:, 0:1], -1.0,
              op0=ALU.mult, op1=ALU.mult)
            W("scalar", sV)

        def chunk_to_fm(c, dest3):
            W("tensor", sV)
            for j in range(DB):
                E("tensor", "transpose", PB3[0:128, 128 * j:128 * (j + 1)],
                  tmb[:, 128 * j:128 * (j + 1)], idb[:])
            W("vector", sT)
            E("vector", "tensor_copy",
              dest3[:, :, 128 * c:128 * (c + 1)],
              PB3[0:128, 0:1024].rearrange("p (a b) -> p a b", a=DB))
            tmb_guard[0] = sT.n

        def ln_chunk_to_fm(c, dest3, src_ap):
            ln_common(src_ap)
            W("vector", sT, tmb_guard[0])
            E("vector", "tensor_scalar", tmb[:], src_ap, rstd[:, 0:1], nmr[:, 0:1],
              op0=ALU.mult, op1=ALU.add)
            chunk_to_fm(c, dest3)

        def post_ln_res(c, y_ap, tmp_ap):
            ln_common(y_ap)
            W("vector", sG)
            E("vector", "tensor_scalar", tmp_ap, y_ap, rstd[:, 0:1], nmr[:, 0:1],
              op0=ALU.mult, op1=ALU.add)
            W("gpsimd", sV)
            E("gpsimd", "tensor_tensor", xres[:, c, :], xres[:, c, :], tmp_ap,
              op=ALU.add)

        E("sync", "dma_start", out=xres[:],
          in_=x_in.rearrange("(a p) d -> p a d", p=128), inc=sLD, inck=16)
        E("vector", "memset", epsv[:], EPS)
        E("vector", "memset", onesf[:], 1.0)
        E("vector", "memset", ones1[:], 1.0)
        E("vector", "memset", SC[:], 0.0)
        E("gpsimd", "memset", idb[:], 0.0)
        E("gpsimd", "memset", idf[:], 0.0)
        E("gpsimd", "affine_select", out=idb[:], in_=idb[:], compare_op=ALU.not_equal,
          fill=1.0, base=0, pattern=[[-1, 128]], channel_multiplier=1)
        E("gpsimd", "affine_select", out=idf[:], in_=idf[:], compare_op=ALU.not_equal,
          fill=1.0, base=0, pattern=[[-1, 128]], channel_multiplier=1)
        W("vector", sLD)
        W("tensor", sG)
        W("scalar", sG)
        for c in range(NCH):
            ln_chunk_to_fm(c, A3, xres[:, c, :])

        for (w_in_, nm) in ((wq0, "q"), (wk0, "k"), (wv0, "v")):
            W("sync", sV)
            E("sync", "dma_start", out=WF[0:64, 0:1024], in_=w_in_, inc=sLD, inck=16)
            W("vector", sLD)
            W("vector", sT)
            E("vector", "tensor_copy", tmb[0:64, :], WF[0:64, 0:1024])
            W("tensor", sV)
            for j in range(DB):
                E("tensor", "transpose", PB3[0:128, 0:64],
                  tmb[0:64, 128 * j:128 * (j + 1)], idb[0:64, 0:64])
                W("vector", sT)
                E("vector", "tensor_copy", w0T[nm][:, 64 * j:64 * (j + 1)],
                  PB3[0:128, 0:64])
                W("tensor", sV)
        tmb_guard[0] = sT.n

        IFC, IPR, SFC, SPR, TFC, TPR = 32, 33, 34, 35, 36, 37
        for ts in range(4):
            sl = slice(512 * ts, 512 * (ts + 1))
            for j in range(DB):
                E("tensor", "matmul", P0[0:64, 0:512], w0T["q"][:, 64 * j:64 * (j + 1)],
                  A3[:, j, sl], start=(j == 0), stop=(j == DB - 1))
            W("scalar", sT)
            E("scalar", "activation", B[0:64, sl], P0[0:64, 0:512], AF.Copy)
            W("tensor", sS)
            for j in range(DB):
                E("tensor", "matmul", P1[0:64, 0:512], w0T["k"][:, 64 * j:64 * (j + 1)],
                  A3[:, j, sl], start=(j == 0), stop=(j == DB - 1))
            W("scalar", sT)
            E("scalar", "activation", kfm[:, sl], P1[0:64, 0:512], AF.Copy)
            W("tensor", sS)
        for c in range(NCH):
            csl = slice(128 * c, 128 * (c + 1))
            for j in range(DB):
                E("tensor", "matmul", P2[0:128, 0:64], A3[:, j, csl],
                  w0T["v"][:, 64 * j:64 * (j + 1)],
                  start=(j == 0), stop=(j == DB - 1))
            W("vector", sT)
            E("vector", "tensor_copy", vtm3[:, c, :], P2[0:128, 0:64])
            W("tensor", sV)

        W("gpsimd", sS)
        E("gpsimd", "dma_start", out=d_q_i.ap(), in_=B[0:64, 0:2048],
          inc=sGD, inck=16)
        W("gpsimd", sGD)
        CC("AllGather", ALU.bypass, [d_q_i.ap()], [d_q_o.ap()], sCC)

        E("sync", "dma_start",
          out=Cf[:, 0:4096].rearrange("p (a d) -> p a d", a=4),
          in_=wfc_sl.rearrange("(a p) d -> p a d", p=128), inc=sLD, inck=16)
        E("sync", "dma_start",
          out=Cf[:, 4096:8192].rearrange("p (a f) -> p a f", a=8),
          in_=wpr_sl.rearrange("(a p) f -> p a f", p=128), inc=sLD, inck=16)
        W("vector", sLD)
        E("vector", "reduce_max", SC[:, 44:45], Cf[:, 0:4096], axis=AX.X,
          apply_absolute_value=True)
        E("vector", "reduce_max", SC[:, 45:46], Cf[:, 4096:8192], axis=AX.X,
          apply_absolute_value=True)
        W("tensor", sV)
        E("tensor", "transpose", P3[0:2, 0:128], SC[:, 44:46], idf[:])
        W("vector", sT)
        E("vector", "reduce_max", SC[0:2, 44:45], P3[0:2, 0:128], axis=AX.X)
        W("tensor", sV)
        E("vector", "memset", SC[0:1, 42:44], 0.0)
        W("gpsimd", sV)
        E("gpsimd", "dma_start", out=d_mx_i.ap()[0, 0:2], in_=SC[0:2, 44:45],
          inc=sGD, inck=16)
        E("gpsimd", "dma_start", out=d_mx_i.ap()[0:1, 2:4], in_=SC[0:1, 42:44],
          inc=sGD, inck=16)
        W("gpsimd", sGD)
        CC("AllReduce", ALU.max, [d_mx_i.ap()], [d_mx_o.ap()], sCC)
        W("sync", sCC)
        E("sync", "dma_start", out=SC[0:1, 16:20], in_=d_mx_o.ap(), inc=sLD, inck=16)
        W("vector", sLD)
        E("vector", "reciprocal", SC[0:1, 20:21], SC[0:1, 16:17])
        E("vector", "reciprocal", SC[0:1, 21:22], SC[0:1, 17:18])
        E("vector", "tensor_scalar", SC[0:1, 24:25], SC[0:1, 20:21], 127.0, None,
          op0=ALU.mult)
        E("vector", "tensor_scalar", SC[0:1, 25:26], SC[0:1, 21:22], 127.0, None,
          op0=ALU.mult)
        E("vector", "tensor_scalar", SC[0:1, 26:27], SC[0:1, 16:17], 1.0 / 127.0,
          None, op0=ALU.mult)
        E("vector", "tensor_scalar", SC[0:1, 27:28], SC[0:1, 17:18], 1.0 / 127.0,
          None, op0=ALU.mult)
        E("vector", "tensor_scalar", SC[0:1, 28:29], SC[0:1, 20:21], 254.0, None,
          op0=ALU.mult)
        E("vector", "tensor_scalar", SC[0:1, 29:30], SC[0:1, 21:22], 254.0, None,
          op0=ALU.mult)
        W("tensor", sV)
        E("tensor", "matmul", P3[0:128, 0:6], ones1[0:1, :], SC[0:1, 24:30],
          start=True, stop=True)
        W("vector", sT)
        E("vector", "tensor_copy", SC[:, 32:38], P3[0:128, 0:6])
        W("tensor", sV)

        rts_guard = {0: 0, 1: 0}

        def quant_slice(base_off, npieces, inv_col, d_tgt, tile_pos):
            pc = 512
            for pi in range(npieces):
                off = base_off + pi * pc
                wap = Cf[:, off:off + pc]
                i32v = SF[:].bitcast(I32)[:, 0:pc]
                cfv = SF[:, 512:1024]
                ddv = WF[:, 0:512]
                W("vector", sT)
                E("vector", "tensor_scalar", i32v, wap, SC[:, inv_col:inv_col + 1],
                  None, op0=ALU.mult)
                E("vector", "tensor_copy", cfv, i32v)
                E("vector", "scalar_tensor_tensor", ddv, wap,
                  SC[:, inv_col:inv_col + 1], cfv, op0=ALU.mult, op1=ALU.subtract)
                E("vector", "tensor_scalar", SF[:, 0:pc], ddv, 0.5, None,
                  op0=ALU.is_ge)
                E("vector", "tensor_tensor", cfv, cfv, SF[:, 0:pc], op=ALU.add)
                E("vector", "tensor_scalar", SF[:, 0:pc], ddv, -0.5, None,
                  op0=ALU.is_le)
                E("vector", "tensor_tensor", cfv, cfv, SF[:, 0:pc], op=ALU.subtract)
                E("vector", "tensor_copy", rbq[:, 0:pc], cfv)
                W("tensor", sV)
                for tj in range(4):
                    rts = rts0 if tj % 2 == 0 else rts1
                    E("tensor", "transpose", PB3[0:128, 0:128],
                      rbq[:, 128 * tj:128 * (tj + 1)], idb[:])
                    W("vector", sT)
                    W("vector", sGD, rts_guard[tj % 2])
                    E("vector", "tensor_copy", rts[:], PB3[0:128, 0:128])
                    W("tensor", sV)
                    W("gpsimd", sV)
                    r0, c0 = tile_pos(pi, tj)
                    E("gpsimd", "dma_start",
                      out=d_tgt.ap()[r0:r0 + 128, c0:c0 + 128], in_=rts[:],
                      inc=sGD, inck=16)
                    rts_guard[tj % 2] = sGD.n

        quant_slice(0, 8, IFC, d_rf_i,
                    lambda pi, tj: (512 * (pi % 2) + 128 * tj, 128 * (pi // 2)))
        quant_slice(4096, 8, IPR, d_rp_i,
                    lambda pi, tj: (128 * tj, 128 * pi))
        W("gpsimd", sGD)
        CC("AllGather", ALU.bypass, [d_rf_i.ap()], [d_rf_o.ap()], sCC)
        CC("AllGather", ALU.bypass, [d_rp_i.ap()], [d_rp_o.ap()], sCC)

        W("sync", sCC, 1)
        for r in range(NC):
            E("sync", "dma_start", out=B[0:64, 2048 * r:2048 * (r + 1)],
              in_=d_q_o.ap()[r], inc=sLD, inck=16)
        W("vector", sV)
        E("vector", "memset", Cf[:, 0:4096], 0.0)
        E("vector", "memset", Cf[:, 4096:8192], 0.0)
        W("tensor", sLD)
        W("tensor", sV)
        W("tensor", sS)

        exp_guard = {0: 0, 1: 0}
        vsl_guard = [0]
        vsl_ready = [0]
        o_guard = {0: 0, 1: 0}
        pv_guard = {0: (sS, 0), 1: (sS, 0)}
        ev_guard = {0: 0, 1: 0}
        vnb_guard = {0: 0, 1: 0}
        vnb_ready = {0: 0, 1: 0}
        ln64_guard = {0: 0, 1: 0}
        z_guard = 0
        pv_done = 0
        for c in range(NCH):
            kl = kfm[:, 128 * c:128 * (c + 1)]
            W("scalar", sT, pv_done)
            W("scalar", sV, z_guard)
            W("vector", sT, pv_done)
            for qs in range(32):
                ps = P0 if qs % 2 == 0 else P1
                qsl = slice(512 * qs, 512 * (qs + 1))
                if qs >= 2:
                    W("tensor", sS, exp_guard[qs % 2])
                E("tensor", "matmul", ps[0:128, 0:512], kl, B[0:64, qsl],
                  start=True, stop=True)
                W("vector", sT)
                E("vector", "reduce_max", SC[:, 96 + qs:97 + qs], ps[0:128, 0:512],
                  axis=AX.X)
                nmc = 44 + (qs % 2)
                E("vector", "tensor_scalar", SC[:, nmc:nmc + 1],
                  SC[:, 96 + qs:97 + qs], -1.0, None, op0=ALU.mult)
                W("scalar", sV)
                E("scalar", "activation", A[:, qsl], ps[0:128, 0:512], AF.Exp,
                  bias=SC[:, nmc:nmc + 1], accum_out=SC[:, 64 + qs:65 + qs])
                exp_guard[qs % 2] = sS.n
            W("vector", sS)
            E("vector", "reduce_max", SC[:, 160:161], SC[:, 96:128], axis=AX.X)
            E("vector", "tensor_scalar", SC[:, 128:160], SC[:, 96:128],
              SC[:, 160:161], None, op0=ALU.subtract)
            W("scalar", sV)
            E("scalar", "activation", SC[:, 128:160], SC[:, 128:160], AF.Exp)
            W("vector", sS)
            E("vector", "tensor_tensor", SC[:, 64:96], SC[:, 64:96],
              SC[:, 128:160], op=ALU.mult)
            E("vector", "reduce_sum", SC[:, 161:162], SC[:, 64:96], axis=AX.X)
            E("vector", "reciprocal", SC[:, 162:163], SC[:, 161:162])
            E("vector", "tensor_scalar", SC[:, 128:160], SC[:, 128:160],
              SC[:, 162:163], None, op0=ALU.mult)
            E("vector", "tensor_copy", evf0[:], vtm3[:, c, :])
            z_guard = sV.n
            W("tensor", sV)
            for qg in range(16):
                for si in range(2):
                    s = 2 * qg + si
                    vb = vnb0 if s % 2 == 0 else vnb1
                    if s >= 2:
                        W("vector", sT, vnb_guard[s % 2])
                    E("vector", "tensor_scalar", evf1[:], evf0[:],
                      SC[:, 128 + s:129 + s], None, op0=ALU.mult)
                    E("vector", "tensor_copy", vb[:], evf1[:])
                W("tensor", sV)
                ps = P2 if qg % 2 == 0 else P3
                if qg >= 2:
                    W("tensor", pv_guard[qg % 2][0], pv_guard[qg % 2][1])
                for i in range(8):
                    qc = 8 * qg + i
                    vb = vnb0 if (qc // 4) % 2 == 0 else vnb1
                    E("tensor", "matmul", ps[0:128, 64 * i:64 * (i + 1)],
                      A[:, 128 * qc:128 * (qc + 1)], vb[:],
                      start=True, stop=True)
                vnb_guard[0] = sT.n
                vnb_guard[1] = sT.n
                W("scalar", sT)
                W("scalar", sG, ev_guard[qg % 2])
                tgt = WF[:, 512 * (qg % 2):512 + 512 * (qg % 2)]
                E("scalar", "activation", tgt, ps[0:128, 0:512], AF.Copy)
                pv_guard[qg % 2] = (sS, sS.n)
                W("gpsimd", sS)
                E("gpsimd", "tensor_tensor", Cf[:, 512 * qg:512 * (qg + 1)],
                  Cf[:, 512 * qg:512 * (qg + 1)], tgt, op=ALU.add)
                ev_guard[qg % 2] = sG.n
            pv_done = sT.n

        W("gpsimd", sG)
        o0v = d_o0_i.ap().rearrange("(a p) e -> p a e", p=128)
        civ = Cf[:, 0:8192].rearrange("p (a e) -> p a e", e=64)
        for g in range(8):
            E("gpsimd", "dma_start", out=o0v[:, 16 * g:16 * (g + 1), :],
              in_=civ[:, 16 * g:16 * (g + 1), :], inc=sGD, inck=16)
        W("gpsimd", sGD)
        CC("ReduceScatter", ALU.add, [d_o0_i.ap()], [d_o0_o.ap()], sCC)
        W("sync", sCC)
        E("sync", "dma_start",
          out=SF[:, 0:1024].rearrange("p (a e) -> p a e", e=64),
          in_=d_o0_o.ap().rearrange("(a p) e -> p a e", p=128),
          inc=sLD, inck=16)
        W("vector", sLD)
        W("scalar", sLD)
        W("vector", sG)
        W("scalar", sG)
        for c in range(NCH):
            o64 = SF[:, 64 * c:64 * (c + 1)]
            E("vector", "reduce_sum", mv[:, 0:1], o64, axis=AX.X)
            E("vector", "tensor_scalar", mv[:, 0:1], mv[:, 0:1], 1.0 / 64.0, None,
              op0=ALU.mult)
            ev = evf0 if c % 2 == 0 else evf1
            if c >= 2:
                W("vector", sG, ln64_guard[c % 2])
            E("vector", "tensor_scalar", ev[:], o64, mv[:, 0:1], None,
              op0=ALU.subtract)
            W("scalar", sV)
            E("scalar", "activation", SF[:, 1024:1088], ev[:], AF.Square,
              accum_out=mv[:, 1:2])
            E("scalar", "activation", sdt[:], mv[:, 1:2], AF.Sqrt, scale=1.0 / 64.0,
              bias=epsv[:])
            W("vector", sS)
            E("vector", "reciprocal", rstd[:], sdt[:])
            E("vector", "tensor_scalar", ev[:], ev[:], rstd[:, 0:1], None,
              op0=ALU.mult)
            W("gpsimd", sV)
            E("gpsimd", "tensor_tensor",
              xres[:, c, :].rearrange("p (a b) -> p a b", b=64),
              xres[:, c, :].rearrange("p (a b) -> p a b", b=64),
              ev[:, None, 0:64].broadcast_to([128, 16, 64]), op=ALU.add)
            ln64_guard[c % 2] = sG.n


        if STAGE >= 2:
            P.bar()
            u_ap = ST[0:16, 0:1024].bitcast(BF16)
            u2_ap = ST[0:16, 1024:2048].bitcast(BF16)
            bprT_c = VQ[0:16, 0:1024]
            bprbc = SF[:, 1024:2048]
            wsA3 = WS[:, 0:512].rearrange("p (a b) -> p a b", a=4)
            wsB3 = WS[:, 512:1024].rearrange("p (a b) -> p a b", a=4)
            wsRp0 = WS[:, 1024:2048]
            wsRp1 = WS[:, 2048:3072]

            W("sync", sCC, 3)
            W("sync", sGD)
            for r in range(NC):
                rfv = d_rf_o.ap()[r].rearrange("(b p) f -> p b f", p=128)
                E("sync", "dma_start", out=C3h[:, 0:4, 512 * r:512 * (r + 1)],
                  in_=rfv[:, 0:4, :], inc=sLD, inck=16)
            W("sync", sCC, 4)

            for c in range(NCH):
                ln_chunk_to_fm(c, A3, xres[:, c, :])

            W("sync", sV)
            E("sync", "dma_start", out=WF[0:16, 0:1024], in_=afc_in, inc=sLD, inck=16)
            W("vector", sLD)
            W("vector", sT, tmb_guard[0])
            E("vector", "tensor_copy", tmb[0:16, :], WF[0:16, 0:1024])
            W("tensor", sV)
            for j in range(DB):
                E("tensor", "transpose", PB3[0:128, 0:16],
                  tmb[0:16, 128 * j:128 * (j + 1)], idb[0:16, 0:16])
                W("vector", sT)
                E("vector", "tensor_copy", afcT[:, 16 * j:16 * (j + 1)],
                  PB3[0:128, 0:16])
                W("tensor", sV)
            if DEBUG:
                W("sync", sV)
                E("sync", "dma_start", out=dbg_i, in_=afcT[:], inc=sSTR, inck=16)
            W("sync", sV)
            W("sync", sT)
            E("sync", "dma_start", out=WF[0:32, 0:128], in_=bfc_in, inc=sLD, inck=16)
            W("tensor", sLD)
            E("tensor", "transpose", P3[0:128, 0:32], WF[0:32, 0:128],
              idf[0:32, 0:32])
            W("vector", sT)
            E("vector", "tensor_copy", bfcv[:], P3[0:128, 0:32])
            W("tensor", sV)
            W("sync", sT)
            E("sync", "dma_start", out=WF[0:1, 0:1024], in_=bpr_in, inc=sLD, inck=16)
            W("tensor", sLD)
            E("tensor", "matmul", P2[0:128, 0:512], ones1[0:1, :],
              WF[0:1, 0:512], start=True, stop=True)
            E("tensor", "matmul", P2[0:128, 512:1024], ones1[0:1, :],
              WF[0:1, 512:1024], start=True, stop=True)
            W("vector", sT)
            W("vector", sS)
            E("vector", "tensor_copy", bprbc[:], P2[0:128, 0:1024])
            W("tensor", sV)
            W("sync", sV)
            W("sync", sT)
            E("sync", "dma_start",
              out=WF[:, 0:512].rearrange("p (a r) -> p a r", a=32),
              in_=bfcL_in.rearrange("(a p) r -> p a r", p=128), inc=sLD, inck=16)
            W("vector", sLD)
            E("vector", "tensor_scalar", WF[:, 0:512], WF[:, 0:512],
              SC[:, TFC:TFC + 1], None, op0=ALU.mult)
            E("vector", "tensor_copy", bfcb[:], WF[:, 0:512])
            for qtr in range(4):
                W("sync", sV)
                W("sync", sT)
                E("sync", "dma_start", out=WF[0:16, 0:1024],
                  in_=apr_in[:, 1024 * qtr:1024 * (qtr + 1)], inc=sLD, inck=16)
                W("vector", sLD)
                W("vector", sT)
                E("vector", "tensor_copy", tmb[0:16, :], WF[0:16, 0:1024])
                W("tensor", sV)
                for j in range(DB):
                    fb = 8 * qtr + j
                    E("tensor", "transpose", PB3[0:128, 0:16],
                      tmb[0:16, 128 * j:128 * (j + 1)], idb[0:16, 0:16])
                    W("vector", sT)
                    E("vector", "tensor_copy", aprT[:, 16 * fb:16 * (fb + 1)],
                      PB3[0:128, 0:16])
                    W("tensor", sV)
            W("sync", sV)
            W("sync", sT)
            E("sync", "dma_start",
              out=WF[:, 0:128].rearrange("p (a r) -> p a r", a=8),
              in_=bprL_in.rearrange("(a p) r -> p a r", p=128), inc=sLD, inck=16)
            W("vector", sLD)
            E("vector", "tensor_scalar", WF[:, 0:128], WF[:, 0:128],
              SC[:, TPR:TPR + 1], None, op0=ALU.mult)
            W("vector", sT, tmb_guard[0])
            E("vector", "tensor_copy", tmb[:, 0:128], WF[:, 0:128])
            W("tensor", sV)
            for j in range(DB):
                E("tensor", "transpose", PB3[0:16, 0:128],
                  tmb[:, 16 * j:16 * (j + 1)], idb[:])
                W("vector", sT)
                E("vector", "tensor_copy", bprT_c[:, 128 * j:128 * (j + 1)],
                  PB3[0:16, 0:128])
                W("tensor", sV)
            tmb_guard[0] = sT.n

            for ts4 in range(4):
                tsl = slice(512 * ts4, 512 * (ts4 + 1))
                for j in range(DB):
                    E("tensor", "matmul", P1[0:16, 0:512],
                      afcT[:, 16 * j:16 * (j + 1)], A3[:, j, tsl],
                      start=(j == 0), stop=(j == DB - 1))
                W("vector", sT)
                E("vector", "tensor_copy", u_ap[:, tsl], P1[0:16, 0:512])
                W("tensor", sV)

            rf_guard = {0: 0, 1: 0}
            rp_guard = {0: 0, 1: 0}
            bfct_guard = {0: 0, 1: 0}
            h1_guard = {0: 0, 1: 0}
            for ts4 in range(4):
                tsl = slice(512 * ts4, 512 * (ts4 + 1))
                W("tensor", sV)
                for fb in range(32):
                    r = fb // 4
                    fs = fb % 4
                    ws = wsA3 if fb % 2 == 0 else wsB3
                    rfv = d_rf_o.ap()[r].rearrange("(b p) f -> p b f", p=128)
                    W("sync", sT, rf_guard[fb % 2])
                    E("sync", "dma_start", out=ws[:, :, :],
                      in_=rfv[:, 4:8, 128 * fs:128 * (fs + 1)], inc=sLD, inck=16)
                    bt_ = bfct0 if fb % 2 == 0 else bfct1
                    W("tensor", sV, bfct_guard[fb % 2])
                    E("tensor", "transpose", PB2[0:16, 1536:1664],
                      bfcb[:, 16 * fb:16 * (fb + 1)], idb[:])
                    W("vector", sT)
                    E("vector", "tensor_copy", bt_, PB2[0:16, 1536:1664])
                    bfct_guard[fb % 2] = sV.n
                    W("tensor", sV)
                    pp = P0 if fb % 2 == 0 else P1
                    ppg = h1_guard[fb % 2]
                    W("tensor", sS, ppg)
                    E("tensor", "matmul", pp[0:16, 0:512]
                      if False else pp[0:128, 0:512],
                      bt_, u_ap[:, tsl], start=True, stop=False)
                    for j in range(4):
                        E("tensor", "matmul", pp[0:128, 0:512],
                          C3h[:, j, 128 * fb:128 * (fb + 1)], A3[:, j, tsl],
                          start=False, stop=False)
                    W("tensor", sLD)
                    for j in range(4):
                        E("tensor", "matmul", pp[0:128, 0:512],
                          ws[:, j, :], A3[:, 4 + j, tsl],
                          start=False, stop=(j == 3))
                    rf_guard[fb % 2] = sT.n
                    W("scalar", sT)
                    E("scalar", "activation", B[:, 512 * fb:512 * (fb + 1)],
                      pp[0:128, 0:512], AF.Gelu, scale=SC[:, SFC:SFC + 1],
                      bias=bfcv[:, fb:fb + 1])
                    h1_guard[fb % 2] = sS.n
                W("tensor", sS)
                for fb in range(32):
                    E("tensor", "matmul", P2[0:16, 0:512],
                      aprT[:, 16 * fb:16 * (fb + 1)],
                      B[:, 512 * fb:512 * (fb + 1)], start=(fb == 0),
                      stop=(fb == 31))
                W("vector", sT)
                E("vector", "tensor_copy", u2_ap[:, tsl], P2[0:16, 0:512])
                W("tensor", sV)
                PQ = [P0, P1, P2, P3]
                for tsub in range(4):
                    t0 = 512 * ts4 + 128 * tsub
                    E("tensor", "matmul", PQ[tsub][0:128, 0:512],
                      u2_ap[:, t0 - 512 * ts4 + 512 * ts4:t0 + 128]
                      if False else u2_ap[:, t0:t0 + 128] if False
                      else u2_ap[:, 512 * ts4 + 128 * tsub:512 * ts4 + 128 * (tsub + 1)],
                      bprT_c[:, 0:512], start=True, stop=False)
                    E("tensor", "matmul", PQ[tsub][0:128, 512:1024],
                      u2_ap[:, 512 * ts4 + 128 * tsub:512 * ts4 + 128 * (tsub + 1)],
                      bprT_c[:, 512:1024], start=True, stop=False)
                for fb in range(32):
                    r = fb // 4
                    fs = fb % 4
                    wsp = wsRp0 if fb % 2 == 0 else wsRp1
                    W("sync", sT, rp_guard[fb % 2])
                    E("sync", "dma_start", out=wsp,
                      in_=d_rp_o.ap()[r, 128 * fs:128 * (fs + 1), :],
                      inc=sLD, inck=16)
                    W("tensor", sLD)
                    for tsub in range(4):
                        h1t = B[:, 512 * fb + 128 * tsub:512 * fb + 128 * (tsub + 1)]
                        E("tensor", "matmul", PQ[tsub][0:128, 0:512], h1t,
                          wsp[:, 0:512], start=False, stop=False)
                        E("tensor", "matmul", PQ[tsub][0:128, 512:1024], h1t,
                          wsp[:, 512:1024], start=False,
                          stop=(fb == 31))
                    rp_guard[fb % 2] = sT.n
                for tsub in range(4):
                    c = 4 * ts4 + tsub
                    W("vector", sT)
                    E("vector", "tensor_scalar", SF[:, 0:1024],
                      PQ[tsub][0:128, 0:1024], SC[:, SPR:SPR + 1], None,
                      op0=ALU.mult)
                    E("vector", "tensor_tensor", SF[:, 0:1024], SF[:, 0:1024],
                      bprbc[:], op=ALU.add)
                    post_ln_res(c, SF[:, 0:1024], WF[:, 0:1024])
                h1_guard = {0: sS.n, 1: sS.n}

        if STAGE >= 3:
            P.bar()
            wcb = WS[:, 0:1024].bitcast(F32)
            STB = WS[:, 1024:2048]
            wqbuf = WS[:, 2048:4096]
            wqb3 = wqbuf.rearrange("p (a j) -> p a j", a=2)
            vqb1 = SFB[:, 2048:3072]
            vq3a = VQ[:].rearrange("p (a t) -> p a t", a=8)
            vq3b = vqb1.rearrange("p (a t) -> p a t", a=8)

            for c in range(NCH):
                ln_chunk_to_fm(c, A3, xres[:, c, :])

            W("sync", sV)
            E("sync", "dma_start",
              out=WF[:, 0:640].rearrange("p (a g) -> p a g", a=DB),
              in_=gw_in.rearrange("(a p) g -> p a g", p=128), inc=sLD, inck=16)
            W("vector", sLD)
            E("vector", "tensor_copy", gwb[:], WF[:, 0:640])
            E("sync", "dma_start", out=WF[0:1, 640:672], in_=gb_in,
              inc=sLD, inck=16)
            E("sync", "dma_start", out=WF[0:1, 672:680], in_=mask_in,
              inc=sLD, inck=16)
            W("tensor", sLD)
            W("tensor", sV)
            E("tensor", "matmul", P3[0:128, 0:32], ones1[0:1, :],
              WF[0:1, 640:672], start=True, stop=True)
            E("tensor", "matmul", P3[0:128, 32:40], ones1[0:1, :],
              WF[0:1, 672:680], start=True, stop=True)
            W("vector", sT)
            E("vector", "memset", gbias[:], 0.0)
            E("vector", "tensor_copy", gbias[:, 16:48], P3[0:128, 0:32])
            E("vector", "tensor_copy", maskb[:], P3[0:128, 32:40])
            E("vector", "memset", wprod[:], 1.0)
            E("vector", "memset", SC[:, 40:41], -5.0)
            W("tensor", sV)

            PK = [P0, P1]; PV_ = [P2, P3]
            kv_guard = [0]
            for g in range(8):
                cpair = (2 * g, 2 * g + 1)
                for wi, (w_in_, pp) in enumerate(((k1, PK), (v1, PV_))):
                    for db in range(8):
                        W("sync", sT, kv_guard[0])
                        W("sync", sV)
                        E("sync", "dma_start", out=WF[:, 0:1024],
                          in_=w_in_[128 * db:128 * (db + 1), :], inc=sLD, inck=16)
                        W("vector", sLD)
                        E("vector", "tensor_copy", wqb3[:, db % 2, :],
                          WF[:, 0:1024])
                        if False:
                            pass
                            E("sync", "dma_start", out=dbg_l[:, 0:512]
                              if False else dbg_l[0:128, 0:512],
                              in_=WF[:, 0:512], inc=sSTR, inck=16)
                            W("tensor", sSTR, sSTR.n)
                            W("vector", sSTR, sSTR.n)
                        W("tensor", sV)
                        for ci, c in enumerate(cpair):
                            for jj in range(2):
                                E("tensor", "matmul",
                                  pp[ci][0:128, 512 * jj:512 * (jj + 1)],
                                  A3[:, db, 128 * c:128 * (c + 1)],
                                  wqb3[:, db % 2, 512 * jj:512 * (jj + 1)],
                                  start=(db == 0), stop=(db == 7),
                                  skip_group_check=True)
                        kv_guard[0] = sT.n
                for ci, c in enumerate(cpair):
                    kf = SF[:, 0:1024]
                    vf = SF[:, 1024:2048]
                    kf3 = kf.rearrange("p (h e) -> p h e", h=16)
                    vf3 = vf.rearrange("p (h e) -> p h e", h=16)
                    W("scalar", sT)
                    W("scalar", sV)
                    W("scalar", sG)
                    if DEBUG and g == 0 and ci == 0:
                        W("vector", sT)
                        E("vector", "tensor_copy", evf0[:], PK[0][0:128, 0:64])
                        W("sync", sV)
                        E("sync", "dma_start", out=dbg_f[:, 0:64], in_=evf0[:],
                          inc=sSTR, inck=16)
                        W("scalar", sSTR, sSTR.n)
                        W("vector", sSTR, sSTR.n)
                    E("scalar", "activation", kf, PK[ci][0:128, 0:1024], AF.Silu)
                    E("scalar", "activation", vf, PV_[ci][0:128, 0:1024], AF.Silu)
                    if DEBUG and g == 0 and ci == 0:
                        W("sync", sS)
                        E("sync", "dma_start", out=dbg_l[:, 512:1024],
                          in_=SF[:, 0:512], inc=sSTR, inck=16)
                        W("vector", sSTR, sSTR.n)
                    W("tensor", sS)
                    for db in range(DB):
                        E("tensor", "matmul", PK[ci][0:128, 0:80],
                          A3[:, db, 128 * c:128 * (c + 1)], gwb3[:, db, :],
                          start=(db == 0), stop=(db == DB - 1))
                    W("vector", sT)
                    gt = SC[:, 96:176]
                    E("vector", "tensor_tensor", gt, PK[ci][0:128, 0:80],
                      gbias[:], op=ALU.add)
                    W("scalar", sV)
                    E("scalar", "activation", gt, gt, AF.Sigmoid)
                    W("vector", sS)
                    E("vector", "tensor_tensor", kf3, kf3,
                      SC[:, None, 96:112].rearrange("p a h -> p h a")
                      .broadcast_to([128, 16, 64]), op=ALU.mult)
                    E("vector", "tensor_tensor", vf3, vf3,
                      SC[:, None, 96:112].rearrange("p a h -> p h a")
                      .broadcast_to([128, 16, 64]), op=ALU.mult)
                    W("gpsimd", sV)
                    E("gpsimd", "tensor_copy", B[:, 1024 * c:1024 * (c + 1)], kf)
                    E("gpsimd", "tensor_copy", C[:, 1024 * c:1024 * (c + 1)], vf)
                    wf3 = WF[:, 0:1024].rearrange("p (h e) -> p h e", h=16)
                    E("vector", "tensor_tensor", wf3, kf3, vf3, op=ALU.mult)
                    E("vector", "reduce_sum",
                      SC[:, 176:192].rearrange("p (a b) -> p a b", b=1), wf3,
                      axis=AX.X)
                    E("vector", "tensor_tensor", wf3, kf3, kf3, op=ALU.mult)
                    E("vector", "reduce_sum",
                      SC[:, 0:16].rearrange("p (a b) -> p a b", b=1), wf3,
                      axis=AX.X)
                    E("vector", "tensor_tensor", wf3, vf3, vf3, op=ALU.mult)
                    E("vector", "reduce_sum",
                      SC[:, 16:32].rearrange("p (a b) -> p a b", b=1), wf3,
                      axis=AX.X)
                    W("scalar", sV)
                    E("scalar", "activation", SC[:, 0:16], SC[:, 0:16], AF.Sqrt)
                    E("scalar", "activation", SC[:, 16:32], SC[:, 16:32], AF.Sqrt)
                    W("vector", sS)
                    E("vector", "tensor_scalar", SC[:, 0:16], SC[:, 0:16], 1e-8,
                      None, op0=ALU.add)
                    E("vector", "tensor_scalar", SC[:, 16:32], SC[:, 16:32], 1e-8,
                      None, op0=ALU.add)
                    E("vector", "tensor_tensor", SC[:, 0:16], SC[:, 0:16],
                      SC[:, 16:32], op=ALU.mult)
                    E("vector", "reciprocal", SC[:, 16:32], SC[:, 0:16])
                    W("scalar", sV)
                    E("scalar", "activation", SC[:, 0:16], SC[:, 176:192], AF.Abs)
                    W("vector", sS)
                    E("vector", "tensor_tensor", SC[:, 0:16], SC[:, 0:16],
                      SC[:, 16:32], op=ALU.mult)
                    W("scalar", sV)
                    E("scalar", "activation", SC[:, 16:32], SC[:, 0:16],
                      AF.Sigmoid, scale=3.0)
                    W("scalar", sS)
                    E("scalar", "activation", SC[:, 0:16], SC[:, 16:32],
                      AF.Sigmoid, scale=10.0, bias=SC[:, 40:41])
                    W("vector", sS)
                    E("vector", "tensor_mul", s2st[:, 16 * c:16 * (c + 1)],
                      SC[:, 0:16], SC[:, 0:16])
                    E("vector", "tensor_scalar", SC[:, 48:64], SC[:, 16:32], 0.3,
                      0.5, op0=ALU.mult, op1=ALU.add)
                    E("vector", "tensor_tensor", afg[:, 16 * c:16 * (c + 1)],
                      SC[:, 48:64], gt[:, 48:64], op=ALU.mult)
                    E("vector", "tensor_scalar", SC[:, 48:64], SC[:, 16:32], -0.3,
                      0.5, op0=ALU.mult, op1=ALU.add)
                    E("vector", "tensor_tensor", asg[:, 16 * c:16 * (c + 1)],
                      SC[:, 48:64], gt[:, 64:80], op=ALU.mult)
                    E("vector", "tensor_scalar", SC[:, 48:64], SC[:, 16:32], -0.1,
                      1.0, op0=ALU.mult, op1=ALU.add)
                    E("vector", "tensor_tensor", SC[:, 48:64], SC[:, 48:64],
                      gt[:, 16:32], op=ALU.mult)
                    E("vector", "tensor_scalar", SC[:, 64:80], SC[:, 16:32], -0.05,
                      1.0, op0=ALU.mult, op1=ALU.add)
                    E("vector", "tensor_tensor", SC[:, 64:80], SC[:, 64:80],
                      gt[:, 32:48], op=ALU.mult)
                    W("tensor", sV)
                    E("tensor", "matmul", PK[ci][0:1, 0:32], onesf[:, 0:1],
                      SC[:, 48:80], start=True, stop=True)
                    W("vector", sT)
                    E("vector", "tensor_scalar", wm[0:1, 32 * c:32 * (c + 1)],
                      PK[ci][0:1, 0:32], 1.0 / 128.0, None, op0=ALU.mult)

            for _ in range(7):
                E("vector", "tensor_mul", wm[:], wm[:], wm[:])
            for c in range(NCH):
                E("vector", "tensor_mul", wprod[:], wprod[:],
                  wm[0:1, 32 * c:32 * (c + 1)])
            W("tensor", sV)
            E("tensor", "matmul", P3[0:128, 0:512], ones1[0:1, :], wm[:],
              start=True, stop=True)
            E("tensor", "matmul", P3[0:128, 512:544], ones1[0:1, :], wprod[:],
              start=True, stop=True)
            W("vector", sT)
            E("vector", "tensor_copy", wcb[:], P3[0:128, 0:512])
            E("vector", "tensor_copy", WF[:, 0:32], P3[0:128, 512:544])
            W("tensor", sV)

            W("tensor", sS)
            E("vector", "memset", stateF[:], 0.0)
            vk_guard = {0: 0, 1: 0}
            def vk_update(c):
                W("vector", sT)
                W("vector", sG)
                E("vector", "tensor_tensor",
                  WF[:, 0:1024].rearrange("p (h e) -> p h e", h=16),
                  C3v[:, c, :, :],
                  s2st[:, 16 * c:16 * (c + 1)]
                  .rearrange("p (h a) -> p h a", a=1)
                  .broadcast_to([128, 16, 64]), op=ALU.mult)
                E("vector", "tensor_copy", tmb[:], WF[:, 0:1024])
                W("tensor", sV, vk_guard[0])
                tm3 = tmb[:].rearrange("p (h e) -> p h e", h=16)
                for r2 in range(2):
                    for h in range(16):
                        par = h % 2
                        pvk = P2 if par == 0 else P3
                        reg = pvk[64 * par:64 * par + 64,
                                  128 * (h // 2) + 64 * r2:
                                  128 * (h // 2) + 64 * r2 + 64]
                        lhs = C3v[:, c, h, :] if r2 == 0 else tm3[:, h, :]
                        E("tensor", "matmul", reg, lhs, B3v[:, c, h, :],
                          start=True, stop=True)
                W("vector", sT)
                for par in range(2):
                    rsl = slice(64 * par, 64 * par + 64)
                    pvk = P2 if par == 0 else P3
                    dview = (wcb[rsl, 32 * c + par:32 * c + 32]
                             .rearrange("p (r h) -> p r h", r=2)[:, :, 0:16:2]
                             if False else
                             wcb[rsl, 32 * c:32 * (c + 1)]
                             .rearrange("p (r q) -> p r q", r=2)[:, :, par:16:2])
                    dv = dview.rearrange("p r h -> p h r")[:, :, :, None]                         .broadcast_to([64, 8, 2, 64])
                    sview = stateF[rsl, :].rearrange("p (h r e) -> p h r e",
                                                     h=8, r=2)
                    E("vector", "tensor_tensor", sview, sview, dv, op=ALU.mult)
                    E("vector", "tensor_tensor", stateF[rsl, :], stateF[rsl, :],
                      pvk[rsl, 0:1024], op=ALU.add)
                vk_guard[0] = sV.n

            for c in range(NCH):
                vk_update(c)
            W("gpsimd", sV)
            E("gpsimd", "dma_start", out=d_st_i.ap()[:, 0:1024], in_=stateF[:],
              inc=sGD, inck=16)
            E("gpsimd", "dma_start", out=d_st_i.ap()[:, 1024:1056],
              in_=WF[:, 0:32], inc=sGD, inck=16)
            W("gpsimd", sGD)
            CC("AllGather", ALU.bypass, [d_st_i.ap()], [d_st_o.ap()], sCC)

            selbuf = SF[:, 1024:2048]
            E("vector", "memset", selbuf, 0.0)
            E("vector", "memset", stateF[:], 0.0)
            W("sync", sCC)
            W("sync", sGD)
            for rk in range(7):
                W("sync", sV)
                E("sync", "dma_start", out=WF[:, 0:1024],
                  in_=d_st_o.ap()[rk, :, 0:1024], inc=sLD, inck=16)
                E("sync", "dma_start", out=SC[:, 96:128],
                  in_=d_st_o.ap()[rk, :, 1024:1056], inc=sLD, inck=16)
                W("vector", sLD)
                for idx in range(32):
                    hp, r2x = idx // 4, (idx % 4) // 2
                    for par in range(2):
                        rsl = slice(64 * par, 64 * par + 64)
                        sl = stateF[rsl, 32 * idx:32 * (idx + 1)]
                        pwc = 96 + 16 * r2x + 2 * hp + par
                        E("vector", "scalar_tensor_tensor", sl, sl,
                          SC[rsl, pwc:pwc + 1],
                          WF[rsl, 32 * idx:32 * (idx + 1)],
                          op0=ALU.mult, op1=ALU.add)
                E("vector", "scalar_tensor_tensor", selbuf, stateF[:],
                  maskb[:, rk:rk + 1], selbuf, op0=ALU.mult, op1=ALU.add)
            E("vector", "tensor_copy", stateF[:], selbuf)

            vk_guard = {0: 0, 1: 0}
            q_guard = [0]
            for cp in range(8):
                c0, c1 = 2 * cp, 2 * cp + 1
                for db in range(8):
                    W("sync", sT, q_guard[0])
                    W("sync", sV)
                    E("sync", "dma_start", out=WF[:, 0:1024],
                      in_=q1[128 * db:128 * (db + 1), :], inc=sLD, inck=16)
                    W("vector", sLD)
                    E("vector", "tensor_copy", wqb3[:, db % 2, :], WF[:, 0:1024])
                    W("tensor", sV)
                    W("tensor", sS)
                    for ci, c in enumerate((c0, c1)):
                        pq = P0 if ci == 0 else P1
                        for jj in range(2):
                            E("tensor", "matmul",
                              pq[0:128, 512 * jj:512 * (jj + 1)],
                              A3[:, db, 128 * c:128 * (c + 1)],
                              wqb3[:, db % 2, 512 * jj:512 * (jj + 1)],
                              start=(db == 0), stop=(db == 7),
                              skip_group_check=True)
                    q_guard[0] = sT.n
                for ci, c in enumerate((c0, c1)):
                    vq3 = vq3a
                    pq = P0 if ci == 0 else P1
                    W("scalar", sT)
                    W("scalar", sV)
                    E("scalar", "activation", SF[:, 0:1024], pq[0:128, 0:1024],
                      AF.Silu)
                    W("vector", sS)
                    W("vector", sT, tmb_guard[0])
                    E("vector", "tensor_copy", tmb[:], SF[:, 0:1024])
                    W("tensor", sV)
                    for jb in range(DB):
                        E("tensor", "transpose",
                          PB2[0:128, 128 * jb:128 * (jb + 1)],
                          tmb[:, 128 * jb:128 * (jb + 1)], idb[:])
                    W("vector", sT)
                    E("vector", "tensor_copy",
                      VQ[:].rearrange("p (a t) -> p a t", a=DB),
                      PB2[0:128, 0:1024].rearrange("p (a t) -> p a t", a=DB))
                    tmb_guard[0] = sT.n
                    W("tensor", sV)
                    vk_update(c)
                    W("vector", sV)
                    E("vector", "tensor_copy", STB, stateF[:])
                    W("tensor", sV)
                    stb4 = STB.rearrange("p (a r e) -> p a r e", a=8, r=2)
                    for h in range(16):
                        par = h % 2
                        qh = vq3[64 * par:64 * par + 64, h // 2, :]
                        E("tensor", "matmul", P2[0:128, 64 * h:64 * (h + 1)], qh,
                          stb4[64 * par:64 * par + 64, h // 2, 0, :],
                          start=True, stop=True)
                        E("tensor", "matmul", P3[0:128, 64 * h:64 * (h + 1)], qh,
                          stb4[64 * par:64 * par + 64, h // 2, 1, :],
                          start=True, stop=True)
                    W("vector", sT)
                    afv = (afg[:, 16 * c:16 * (c + 1)]
                           .rearrange("p (h a) -> p h a", a=1)
                           .broadcast_to([128, 16, 64]))
                    asv = (asg[:, 16 * c:16 * (c + 1)]
                           .rearrange("p (h a) -> p h a", a=1)
                           .broadcast_to([128, 16, 64]))
                    p2v = P2[0:128, 0:1024].rearrange("p (h e) -> p h e", h=16)
                    p3v = P3[0:128, 0:1024].rearrange("p (h e) -> p h e", h=16)
                    sfv = SF[:, 0:1024].rearrange("p (h e) -> p h e", h=16)
                    wfv = WF[:, 0:1024].rearrange("p (h e) -> p h e", h=16)
                    E("vector", "tensor_tensor", sfv, p2v, afv, op=ALU.mult)
                    E("vector", "tensor_tensor", wfv, p3v, asv, op=ALU.mult)
                    E("vector", "tensor_tensor", SF[:, 0:1024], SF[:, 0:1024],
                      WF[:, 0:1024], op=ALU.add)
                    W("vector", sT, tmb_guard[0])
                    E("vector", "tensor_copy", tmb[:], SF[:, 0:1024])
                    chunk_to_fm(c, A3)
            PQ4 = [P0, P1, P2, P3]
            wo_guard = [0]
            for grp in range(4):
                for db in range(8):
                    W("sync", sT, wo_guard[0])
                    W("sync", sV)
                    E("sync", "dma_start", out=WF[:, 0:1024],
                      in_=o1[128 * db:128 * (db + 1), :], inc=sLD, inck=16)
                    W("vector", sLD)
                    E("vector", "tensor_copy", wqb3[:, db % 2, :], WF[:, 0:1024])
                    W("tensor", sV)
                    for ci in range(4):
                        c = 4 * grp + ci
                        for jj in range(2):
                            E("tensor", "matmul",
                              PQ4[ci][0:128, 512 * jj:512 * (jj + 1)],
                              A3[:, db, 128 * c:128 * (c + 1)],
                              wqb3[:, db % 2, 512 * jj:512 * (jj + 1)],
                              start=(db == 0), stop=(db == 7),
                              skip_group_check=True)
                    wo_guard[0] = sT.n
                for ci in range(4):
                    c = 4 * grp + ci
                    W("vector", sT)
                    W("vector", sS)
                    E("vector", "tensor_copy", SF[:, 0:1024],
                      PQ4[ci][0:128, 0:1024])
                    post_ln_res(c, SF[:, 0:1024], SF[:, 1024:2048])

        if STAGE >= 4:
            P.bar()
            yn_sb = Cf[0:1, 0:8192]
            ynb = Cf[0:64, 4096:5120]
            bexp_sb = Cf[0:64, 5120:6144]
            ynbf = C[0:64, 12288:13312]
            B3p = B[:].rearrange("p (a t) -> p a t", a=DB)
            for c in range(NCH):
                ln_chunk_to_fm(c, A3, xres[:, c, :])
            for db in range(DB):
                W("scalar", sV)
                E("scalar", "activation", C[:, 0:2048], A3[:, db, :], AF.Copy,
                  accum_out=Sfm[:, db:db + 1])
            W("gpsimd", sS)
            E("gpsimd", "dma_start", out=d_S_i.ap(), in_=Sfm[:], inc=sGD, inck=16)
            W("gpsimd", sGD)
            CC("AllReduce", ALU.add, [d_S_i.ap()], [d_S_o.ap()], sCC)
            W("sync", sCC)
            E("sync", "dma_start", out=Sfm[:], in_=d_S_o.ap(), inc=sLD, inck=16)
            W("vector", sLD)
            E("vector", "tensor_copy", Sbf[:], Sfm[:])
            yn_guard = [0]
            for q4 in range(4):
                for db in range(DB):
                    for half in range(2):
                        W("sync", sT, yn_guard[0])
                        W("sync", sV)
                        E("sync", "dma_start", out=WF[:, 0:1024],
                          in_=wexp_in.rearrange("n (a p) e -> n p a e", p=128)[
                              2 * q4 + half, :, db, :], inc=sLD, inck=16)
                        W("vector", sLD)
                        E("vector", "tensor_copy",
                          wqb3[:, half, :], WF[:, 0:1024])
                    W("tensor", sV)
                    W("tensor", sS)
                    pq_regions = [(P0, 0), (P0, 512), (P1, 0), (P1, 512)]
                    for j in range(4):
                        pt, c0_ = pq_regions[j]
                        E("tensor", "matmul", pt[0:1, c0_:c0_ + 512],
                          Sbf[:, db:db + 1],
                          wqbuf[:, 512 * j:512 * (j + 1)],
                          start=(db == 0), stop=(db == DB - 1),
                          skip_group_check=True)
                    yn_guard[0] = sT.n
                W("vector", sT)
                for j in range(4):
                    pt, c0_ = [(P0, 0), (P0, 512), (P1, 0), (P1, 512)][j]
                    g = 2048 * q4 + 512 * j
                    E("vector", "tensor_copy", yn_sb[0:1, g:g + 512],
                      pt[0:1, c0_:c0_ + 512])
            W("gpsimd", sV)
            E("gpsimd", "dma_start", out=d_yn_i.ap().rearrange("a b -> (a b)"),
              in_=yn_sb, inc=sGD, inck=16)
            W("gpsimd", sGD)
            CC("AllGather", ALU.bypass, [d_yn_i.ap()], [d_yn_o.ap()], sCC)
            W("sync", sCC)
            for r in range(NC):
                E("sync", "dma_start", out=ynb[8 * r:8 * (r + 1), :],
                  in_=d_yn_o.ap()[r].rearrange("a (n e) -> (a n) e", e=1024),
                  inc=sLD, inck=16)
            E("sync", "dma_start", out=bexp_sb, in_=bexp_in, inc=sLD, inck=16)
            W("vector", sLD)
            E("vector", "tensor_tensor", ynb, ynb, bexp_sb, op=ALU.add)
            E("vector", "tensor_copy", ynbf, ynb)
            for db in range(DB):
                for half in range(2):
                    W("sync", sV)
                    E("sync", "dma_start", out=WF[:, 0:1024],
                      in_=phi_in[128 * db:128 * (db + 1),
                                 1024 * half:1024 * (half + 1)],
                      inc=sLD, inck=16)
                    W("vector", sLD)
                    E("vector", "tensor_copy",
                      B3p[:, db, 1024 * half:1024 * (half + 1)], WF[:, 0:1024])
            W("tensor", sV)
            lg_guard = [0]
            for c in range(NCH):
                csl = slice(128 * c, 128 * (c + 1))
                W("tensor", sV, lg_guard[0])
                for db in range(DB):
                    for j in range(4):
                        pt = P2 if j < 2 else P3
                        c0_ = 512 * (j % 2)
                        E("tensor", "matmul", pt[0:128, c0_:c0_ + 512],
                          A3[:, db, csl], B3p[:, db, 512 * j:512 * (j + 1)],
                          start=(db == 0), stop=(db == DB - 1),
                          skip_group_check=True)
                W("vector", sT)
                W("vector", sG)
                E("vector", "tensor_copy", SF[:, 0:1024], P2[0:128, 0:1024])
                E("vector", "tensor_copy", SF[:, 1024:2048], P3[0:128, 0:1024])
                E("vector", "reduce_max", SC[:, 64:65], SF[:, 0:2048], axis=AX.X)
                E("vector", "tensor_scalar", SC[:, 65:66], SC[:, 64:65], -1.0,
                  None, op0=ALU.mult)
                W("scalar", sV)
                E("scalar", "activation", Cf[:, 0:2048], SF[:, 0:2048], AF.Exp,
                  bias=SC[:, 65:66], accum_out=SC[:, 66:67])
                W("vector", sS)
                E("vector", "reciprocal", SC[:, 67:68], SC[:, 66:67])
                E("vector", "reduce_sum",
                  SC[:, 0:64].rearrange("p (a b) -> p a b", b=1),
                  Cf[:, 0:2048].rearrange("p (a b) -> p a b", b=32), axis=AX.X)
                E("vector", "tensor_scalar", SC[:, 0:64], SC[:, 0:64],
                  SC[:, 67:68], None, op0=ALU.mult)
                W("vector", sT, tmb_guard[0])
                E("vector", "tensor_copy", tmb[:, 0:64], SC[:, 0:64])
                W("tensor", sV)
                cht = chT0 if c % 2 == 0 else chT1
                E("tensor", "transpose", PB2[0:64, 1536:1664], tmb[:, 0:64],
                  idb[:])
                tmb_guard[0] = sT.n
                W("vector", sT)
                E("vector", "tensor_copy", cht, PB2[0:64, 1536:1664])
                W("tensor", sV)
                E("tensor", "matmul", P0[0:128, 0:512], cht, ynbf[:, 0:512],
                  start=True, stop=True)
                E("tensor", "matmul", P0[0:128, 512:1024], cht, ynbf[:, 512:1024],
                  start=True, stop=True)
                W("vector", sT)
                E("vector", "tensor_copy", SF[:, 0:1024], P0[0:128, 0:1024])
                lg_guard[0] = sV.n
                post_ln_res(c, SF[:, 0:1024], SF[:, 1024:2048])
        P.bar()
        W("sync", sG)
        W("sync", sV)
        W("sync", sS)
        W("sync", sT)
        E("sync", "dma_start", out=y_out.rearrange("(a p) d -> p a d", p=128),
          in_=xres[:], inc=sSTR, inck=16)
        if DEBUG:
            E("vector", "tensor_copy", WF[:, 0:512], A[:, 0:512])
            E("vector", "tensor_copy", WF[:, 512:1024],
              vtm3[:, 0:8, :].rearrange("p a b -> p (a b)"))
            W("sync", sV)
            E("sync", "dma_start", out=dbg_sc, in_=SC[:], inc=sSTR, inck=16)
            E("sync", "dma_start", out=dbg_a, in_=WF[:, 0:512], inc=sSTR, inck=16)
            E("sync", "dma_start", out=dbg_b, in_=WF[:, 512:1024], inc=sSTR, inck=16)
            E("sync", "dma_start", out=dbg_c, in_=Cf[:, 0:512], inc=sSTR, inck=16)
            if STAGE >= 3:
                E("sync", "dma_start", out=dbg_j, in_=stateF[:], inc=sSTR, inck=16)
                W("sync", sLD)
                E("vector", "tensor_copy", WF[:, 0:512], C[:, 0:512])
                E("vector", "tensor_copy", WF[:, 512:768], s2st[:])
                E("vector", "tensor_copy", WF[:, 768:1024], afg[:])
                W("sync", sV)
                E("sync", "dma_start", out=dbg_k, in_=WF[:, 0:1024],
                  inc=sSTR, inck=16)
        W("gpsimd", sSTR, sSTR.n)

        with nc.Block() as block:
            for name in ENGINES:
                def engine_fn(eng, name=name):
                    for it_name, thunk in P.items:
                        if it_name == name:
                            thunk(eng)
                getattr(block, name)(engine_fn)
    return nc


_NC_CACHE = {}
_LAST_RES = None


def kernel(**inputs):
    x = np.ascontiguousarray(np.asarray(inputs["L0_attn_x"], np.float32)[0])
    in_maps = []
    for c in range(NC):
        m = {
            "x_sh": np.ascontiguousarray(x[TPC * c:TPC * (c + 1)]),
            "wq0": np.ascontiguousarray(inputs["L0_attn_Wq"]),
            "wk0": np.ascontiguousarray(inputs["L0_attn_Wk"]),
            "wv0": np.ascontiguousarray(inputs["L0_attn_Wv"]),
            "wfc_sl": np.ascontiguousarray(
                inputs["L0_ffn_W_fc"][512 * c:512 * (c + 1)]),
            "bfc": np.ascontiguousarray(inputs["L0_ffn_b_fc"].reshape(32, 128)),
            "afc": np.ascontiguousarray(inputs["L0_ffn_A_fc"]),
            "bfcL": np.ascontiguousarray(inputs["L0_ffn_B_fc"]),
            "wpr_sl": np.ascontiguousarray(
                inputs["L0_ffn_W_proj"][:, 512 * c:512 * (c + 1)]),
            "bpr": np.ascontiguousarray(inputs["L0_ffn_b_proj"].reshape(1, D)),
            "apr": np.ascontiguousarray(inputs["L0_ffn_A_proj"]),
            "bprL": np.ascontiguousarray(inputs["L0_ffn_B_proj"]),
            "q1w": np.ascontiguousarray(inputs["L1_attn_Wq"]),
            "k1w": np.ascontiguousarray(inputs["L1_attn_Wk"]),
            "v1w": np.ascontiguousarray(inputs["L1_attn_Wv"]),
            "o1w": np.ascontiguousarray(inputs["L1_attn_Wo"]),
            "gw": np.ascontiguousarray(np.concatenate(
                [inputs["L1_attn_Wb"], inputs["L1_attn_Wfd"],
                 inputs["L1_attn_Wsd"], inputs["L1_attn_Wfg"],
                 inputs["L1_attn_Wsg"]], axis=1)),
            "gbias": np.ascontiguousarray(np.stack(
                [inputs["L1_attn_fd_bias"], inputs["L1_attn_sd_bias"]])),
            "phi": np.ascontiguousarray(inputs["L1_ffn_phi"].reshape(D, 2048)),
            "wexp": np.ascontiguousarray(inputs["L1_ffn_W"][8 * c:8 * (c + 1)]),
            "bexp": np.ascontiguousarray(inputs["L1_ffn_b"]),
            "mask": (np.eye(8, dtype=np.float32)[c - 1:c] if c > 0
                     else np.zeros((1, 8), np.float32)),
        }
        in_maps.append(m)
    if "prog" not in _NC_CACHE:
        _NC_CACHE["prog"] = build()
    res = run_bass_kernel_spmd(_NC_CACHE["prog"], in_maps, core_ids=list(range(NC)))
    global _LAST_RES
    _LAST_RES = res
    y = np.concatenate([np.asarray(r["y"], np.float32) for r in res.results], axis=0)
    return y.reshape(1, 16384, 1024)


# version 06
# speedup vs baseline: 1.0220x; 1.0220x over previous
import numpy as np
import concourse.bass as bass
import concourse.mybir as mybir
from concourse.bass_utils import run_bass_kernel_spmd
from contextlib import ExitStack

F32 = mybir.dt.float32
BF16 = mybir.dt.bfloat16
I32 = mybir.dt.int32
AF = mybir.ActivationFunctionType
ALU = mybir.AluOpType
AX = mybir.AxisListType

NC = 8
TPC = 2048
NCH = 16
D = 1024
DB = 8
SHIFT = 16.0
EPS = 1e-5
STAGE = 4
DEBUG = False
DEDUPE = True

ENGINES = ("sync", "vector", "scalar", "tensor", "gpsimd")


class Sem:
    def __init__(self, nc, ctx, name):
        self.h = ctx.enter_context(nc.semaphore(name))
        self.n = 0


class Prog:

    def __init__(self, sems):
        self.items = []
        self.own = sems
        self.last_wait = {}

    def E(self, eng, meth, *args, inc=None, inck=1, **kw):
        own = self.own.get(eng)
        is_dma = meth.startswith("dma_start") or meth == "collective_compute"
        if own is not None and not is_dma:
            self.W(eng, own)
            if inc is None:
                inc, inck = own, 1
        if inc is not None:
            inc.n += inck
        def thunk(e, meth=meth, args=args, kw=kw, inc=inc, inck=inck):
            ins = getattr(e, meth)(*args, **kw)
            if inc is not None:
                ins.then_inc(inc.h, inck)
        self.items.append((eng, thunk))

    def W(self, eng, sem, val=None):
        v = sem.n if val is None else val
        key = (eng, id(sem))
        if DEDUPE and self.last_wait.get(key, -1) >= v:
            return
        self.last_wait[key] = v
        def thunk(e, sem=sem, v=v):
            e.wait_ge(sem.h, v)
        self.items.append((eng, thunk))

    def CC(self, kind, op, ins_, outs_, inc):
        inc.n += 1
        def thunk(e, kind=kind, op=op, ins_=ins_, outs_=outs_, inc=inc):
            e.collective_compute(kind, op, replica_groups=[list(range(NC))],
                                 ins=ins_, outs=outs_).then_inc(inc.h, 1)
        self.items.append(("gpsimd", thunk))

    def bar(self):
        for e in ("vector", "scalar", "tensor", "gpsimd"):
            for s in self.own.values():
                self.W(e, s)


def build():
    nc = bass.Bass("TRN2", target_bir_lowering=False)

    def din(name, shape, dtype=F32):
        return nc.dram_tensor(name, shape, dtype, kind="ExternalInput").ap()

    x_in = din("x_sh", [TPC, D])
    wq0 = din("wq0", [64, D]); wk0 = din("wk0", [64, D]); wv0 = din("wv0", [64, D])
    wfc_sl = din("wfc_sl", [512, D])
    bfc_in = din("bfc", [32, 128])
    afc_in = din("afc", [16, D])
    bfcL_in = din("bfcL", [4096, 16])
    wpr_sl = din("wpr_sl", [D, 512])
    bpr_in = din("bpr", [1, D])
    apr_in = din("apr", [16, 4096])
    bprL_in = din("bprL", [D, 16])
    q1 = din("q1w", [D, D]); k1 = din("k1w", [D, D]); v1 = din("v1w", [D, D])
    o1 = din("o1w", [D, D])
    gw_in = din("gw", [D, 80])
    gb_in = din("gbias", [2, 16])
    phi_in = din("phi", [D, 2048])
    wexp_in = din("wexp", [8, D, D])
    bexp_in = din("bexp", [64, D])
    mask_in = din("mask", [1, 8])
    y_out = nc.dram_tensor("y", [TPC, D], F32, kind="ExternalOutput").ap()
    dbg_sc = nc.dram_tensor("dbg_sc", [128, 192], F32, kind="ExternalOutput").ap()
    dbg_a = nc.dram_tensor("dbg_a", [128, 512], F32, kind="ExternalOutput").ap()
    dbg_b = nc.dram_tensor("dbg_b", [128, 512], F32, kind="ExternalOutput").ap()
    dbg_c = nc.dram_tensor("dbg_c", [128, 512], F32, kind="ExternalOutput").ap()
    dbg_d = nc.dram_tensor("dbg_d", [128, 512], F32, kind="ExternalOutput").ap()
    dbg_e = nc.dram_tensor("dbg_e", [128, 512], F32, kind="ExternalOutput").ap()
    dbg_f = nc.dram_tensor("dbg_f", [128, 512], F32, kind="ExternalOutput").ap()
    dbg_g = nc.dram_tensor("dbg_g", [128, 512], BF16, kind="ExternalOutput").ap()
    dbg_h = nc.dram_tensor("dbg_h", [64, 2048], F32, kind="ExternalOutput").ap()
    dbg_i = nc.dram_tensor("dbg_i", [128, 128], BF16, kind="ExternalOutput").ap()
    dbg_j = nc.dram_tensor("dbg_j", [128, 1024], F32, kind="ExternalOutput").ap()
    dbg_k = nc.dram_tensor("dbg_k", [128, 1024], F32, kind="ExternalOutput").ap()
    dbg_l = nc.dram_tensor("dbg_l", [128, 1024], F32, kind="ExternalOutput").ap()
    dbg_m = nc.dram_tensor("dbg_m", [128, 512], BF16, kind="ExternalOutput").ap()

    d_mx_i = nc.dram_tensor("d_mx_i", [1, 4], F32, kind="Internal")
    d_mx_o = nc.dram_tensor("d_mx_o", [1, 4], F32, kind="Internal")
    d_q_i = nc.dram_tensor("d_q_i", [64, TPC], BF16, kind="Internal")
    d_q_o = nc.dram_tensor("d_q_o", [NC, 64, TPC], BF16, kind="Internal")
    d_rf_i = nc.dram_tensor("d_rf_i", [D, 512], BF16, kind="Internal")
    d_rf_o = nc.dram_tensor("d_rf_o", [NC, D, 512], BF16, kind="Internal")
    d_rp_i = nc.dram_tensor("d_rp_i", [512, D], BF16, kind="Internal")
    d_rp_o = nc.dram_tensor("d_rp_o", [NC, 512, D], BF16, kind="Internal")
    d_o0_i = nc.dram_tensor("d_o0_i", [NC * TPC, 64], F32, kind="Internal")
    d_o0_o = nc.dram_tensor("d_o0_o", [TPC, 64], F32, kind="Internal")
    d_st_i = nc.dram_tensor("d_st_i", [128, 1056], F32, kind="Internal")
    d_st_o = nc.dram_tensor("d_st_o", [NC, 128, 1056], F32, kind="Internal")
    d_S_i = nc.dram_tensor("d_S_i", [128, 8], F32, kind="Internal")
    d_S_o = nc.dram_tensor("d_S_o", [128, 8], F32, kind="Internal")
    d_yn_i = nc.dram_tensor("d_yn_i", [2, 4096], F32, kind="Internal")
    d_yn_o = nc.dram_tensor("d_yn_o", [NC, 2, 4096], F32, kind="Internal")

    with ExitStack() as ctx:
        sb = lambda n_, s_, d_=F32: ctx.enter_context(nc.sbuf_tensor(n_, s_, d_))
        pst = lambda n_, s_, d_=F32: ctx.enter_context(nc.psum_tensor(n_, s_, d_))

        xres = sb("xres", [128, NCH, D])
        A = sb("A", [128, 16384], BF16)
        B = sb("B", [128, 16384], BF16)
        C = sb("C", [128, 16384], BF16)
        SF = sb("SF", [128, 2048])
        ST = sb("ST", [64, 2048])
        WS = sb("WS", [128, 4096], BF16)
        WF = sb("WF", [128, 1024])
        tmb = sb("tmb", [128, D], BF16)
        VQ = sb("VQ", [128, 1024], BF16)
        stateF = sb("stateF", [128, 1024])
        s2st = sb("s2st", [128, 256]); afg = sb("afg", [128, 256])
        asg = sb("asg", [128, 256])
        SC = sb("SC", [128, 192])
        idb = sb("idb", [128, 128], BF16)
        idf = sb("idf", [128, 128])
        epsv = sb("epsv", [128, 1])
        onesf = sb("onesf", [128, 1])
        ones1 = sb("ones1", [1, 128])
        bn6 = sb("bn6", [128, 2, 6])
        mv = sb("mv", [128, 2])
        sdt = sb("sdt", [128, 1]); rstd = sb("rstd", [128, 1]); nmr = sb("nmr", [128, 1])
        evf0 = sb("evf0", [128, 64]); evf1 = sb("evf1", [128, 64])
        vnb0 = sb("vnb0", [128, 64], BF16); vnb1 = sb("vnb1", [128, 64], BF16)
        rts0 = sb("rts0", [128, 128], BF16); rts1 = sb("rts1", [128, 128], BF16)
        afcT = sb("afcT", [128, 128], BF16)
        bfct0f = sb("bfct0", [16, 128], BF16); bfct1f = sb("bfct1", [16, 128], BF16)
        bfct0 = bfct0f[:]; bfct1 = bfct1f[:]
        bfcv = sb("bfcv", [128, 32])
        gwb = sb("gwb", [128, 640], BF16)
        gbias = sb("gbias_bc", [128, 80])
        wprod = sb("wprod", [1, 32])
        maskb = sb("maskb", [128, 8])
        vsl = sb("vsl", [128, 64], BF16)
        Sfm = sb("Sfm", [128, 8]); Sbf = sb("Sbf", [128, 8], BF16)

        P0 = pst("P0", [128, 1024]); P1 = pst("P1", [128, 1024])
        P2 = pst("P2", [128, 1024]); P3 = pst("P3", [128, 1024])
        PB2 = P2[:].bitcast(BF16)
        PB3 = P3[:].bitcast(BF16)

        sV = Sem(nc, ctx, "sV"); sS = Sem(nc, ctx, "sS")
        sT = Sem(nc, ctx, "sT"); sG = Sem(nc, ctx, "sG")
        sLD = Sem(nc, ctx, "sLD"); sGD = Sem(nc, ctx, "sGD")
        sVD = Sem(nc, ctx, "sVD"); sSTR = Sem(nc, ctx, "sSTR")
        sCC = Sem(nc, ctx, "sCC")

        P = Prog({"vector": sV, "scalar": sS, "tensor": sT, "gpsimd": sG})
        E, W, CC = P.E, P.W, P.CC

        Af = A[:].bitcast(F32)
        Bf = B[:].bitcast(F32)
        Cf = C[:].bitcast(F32)
        A3 = A[:].rearrange("p (a b) -> p a b", a=DB)
        C3h = C[:].rearrange("p (a b) -> p a b", a=4)
        w0T = {"q": WS[:, 0:512], "k": WS[:, 512:1024], "v": WS[:, 1024:1536]}
        rbq = WS[:, 1536:2560]
        wm = ST[0:1, 1536:2048]
        SFB = SF[:].bitcast(BF16)
        kfm = SFB[0:64, 2048:4096]
        vtm3 = tmb[:].rearrange("p (a b) -> p a b", a=NCH)
        aprT = WS[:, 3072:3584]
        bfcb = WS[:, 3584:4096]
        chT0 = VQ[0:64, 0:128]; chT1 = VQ[0:64, 128:256]
        gwb3 = gwb[:].rearrange("p (a g) -> p a g", a=DB)
        st4 = stateF[:].rearrange("p (a r e) -> p a r e", a=8, r=2)
        C3v = C[:].rearrange("p (c h e) -> p c h e", c=NCH, h=16)
        B3v = B[:].rearrange("p (c h e) -> p c h e", c=NCH, h=16)

        tmb_guard = [0]

        def ln_common(src_ap):
            E("vector", "bn_stats", bn6[:, 0, :], src_ap[:, 0:512])
            E("vector", "bn_stats", bn6[:, 1, :], src_ap[:, 512:1024])
            E("vector", "bn_aggr", mv[:], bn6[:])
            W("scalar", sV)
            E("scalar", "activation", sdt[:], mv[:, 1:2], AF.Sqrt, bias=epsv[:])
            W("vector", sS)
            E("vector", "reciprocal", rstd[:], sdt[:])
            E("vector", "tensor_scalar", nmr[:], mv[:, 0:1], rstd[:, 0:1], -1.0,
              op0=ALU.mult, op1=ALU.mult)
            W("scalar", sV)

        def chunk_to_fm(c, dest3):
            W("tensor", sV)
            for j in range(DB):
                E("tensor", "transpose", PB3[0:128, 128 * j:128 * (j + 1)],
                  tmb[:, 128 * j:128 * (j + 1)], idb[:])
            W("vector", sT)
            E("vector", "tensor_copy",
              dest3[:, :, 128 * c:128 * (c + 1)],
              PB3[0:128, 0:1024].rearrange("p (a b) -> p a b", a=DB))
            tmb_guard[0] = sT.n

        def ln_chunk_to_fm(c, dest3, src_ap):
            ln_common(src_ap)
            W("vector", sT, tmb_guard[0])
            E("vector", "tensor_scalar", tmb[:], src_ap, rstd[:, 0:1], nmr[:, 0:1],
              op0=ALU.mult, op1=ALU.add)
            chunk_to_fm(c, dest3)

        def post_ln_res(c, y_ap, tmp_ap):
            ln_common(y_ap)
            W("vector", sG)
            E("vector", "tensor_scalar", tmp_ap, y_ap, rstd[:, 0:1], nmr[:, 0:1],
              op0=ALU.mult, op1=ALU.add)
            W("gpsimd", sV)
            E("gpsimd", "tensor_tensor", xres[:, c, :], xres[:, c, :], tmp_ap,
              op=ALU.add)

        E("sync", "dma_start", out=xres[:],
          in_=x_in.rearrange("(a p) d -> p a d", p=128), inc=sLD, inck=16)
        E("vector", "memset", epsv[:], EPS)
        E("vector", "memset", onesf[:], 1.0)
        E("vector", "memset", ones1[:], 1.0)
        E("vector", "memset", SC[:], 0.0)
        E("gpsimd", "memset", idb[:], 0.0)
        E("gpsimd", "memset", idf[:], 0.0)
        E("gpsimd", "affine_select", out=idb[:], in_=idb[:], compare_op=ALU.not_equal,
          fill=1.0, base=0, pattern=[[-1, 128]], channel_multiplier=1)
        E("gpsimd", "affine_select", out=idf[:], in_=idf[:], compare_op=ALU.not_equal,
          fill=1.0, base=0, pattern=[[-1, 128]], channel_multiplier=1)
        W("vector", sLD)
        W("tensor", sG)
        W("scalar", sG)
        for c in range(NCH):
            ln_chunk_to_fm(c, A3, xres[:, c, :])

        for (w_in_, nm) in ((wq0, "q"), (wk0, "k"), (wv0, "v")):
            W("sync", sV)
            E("sync", "dma_start", out=WF[0:64, 0:1024], in_=w_in_, inc=sLD, inck=16)
            W("vector", sLD)
            W("vector", sT)
            E("vector", "tensor_copy", tmb[0:64, :], WF[0:64, 0:1024])
            W("tensor", sV)
            for j in range(DB):
                E("tensor", "transpose", PB3[0:128, 0:64],
                  tmb[0:64, 128 * j:128 * (j + 1)], idb[0:64, 0:64])
                W("vector", sT)
                E("vector", "tensor_copy", w0T[nm][:, 64 * j:64 * (j + 1)],
                  PB3[0:128, 0:64])
                W("tensor", sV)
        tmb_guard[0] = sT.n

        IFC, IPR, SFC, SPR, TFC, TPR = 32, 33, 34, 35, 36, 37
        for ts in range(4):
            sl = slice(512 * ts, 512 * (ts + 1))
            for j in range(DB):
                E("tensor", "matmul", P0[0:64, 0:512], w0T["q"][:, 64 * j:64 * (j + 1)],
                  A3[:, j, sl], start=(j == 0), stop=(j == DB - 1))
            W("scalar", sT)
            E("scalar", "activation", B[0:64, sl], P0[0:64, 0:512], AF.Copy)
            W("tensor", sS)
            for j in range(DB):
                E("tensor", "matmul", P1[0:64, 0:512], w0T["k"][:, 64 * j:64 * (j + 1)],
                  A3[:, j, sl], start=(j == 0), stop=(j == DB - 1))
            W("scalar", sT)
            E("scalar", "activation", kfm[:, sl], P1[0:64, 0:512], AF.Copy)
            W("tensor", sS)
        for c in range(NCH):
            csl = slice(128 * c, 128 * (c + 1))
            for j in range(DB):
                E("tensor", "matmul", P2[0:128, 0:64], A3[:, j, csl],
                  w0T["v"][:, 64 * j:64 * (j + 1)],
                  start=(j == 0), stop=(j == DB - 1))
            W("vector", sT)
            E("vector", "tensor_copy", vtm3[:, c, :], P2[0:128, 0:64])
            W("tensor", sV)

        W("gpsimd", sS)
        E("gpsimd", "dma_start", out=d_q_i.ap(), in_=B[0:64, 0:2048],
          inc=sGD, inck=16)
        W("gpsimd", sGD)
        CC("AllGather", ALU.bypass, [d_q_i.ap()], [d_q_o.ap()], sCC)

        E("sync", "dma_start",
          out=Cf[:, 0:4096].rearrange("p (a d) -> p a d", a=4),
          in_=wfc_sl.rearrange("(a p) d -> p a d", p=128), inc=sLD, inck=16)
        E("sync", "dma_start",
          out=Cf[:, 4096:8192].rearrange("p (a f) -> p a f", a=8),
          in_=wpr_sl.rearrange("(a p) f -> p a f", p=128), inc=sLD, inck=16)
        W("vector", sLD)
        E("vector", "reduce_max", SC[:, 44:45], Cf[:, 0:4096], axis=AX.X,
          apply_absolute_value=True)
        E("vector", "reduce_max", SC[:, 45:46], Cf[:, 4096:8192], axis=AX.X,
          apply_absolute_value=True)
        W("tensor", sV)
        E("tensor", "transpose", P3[0:2, 0:128], SC[:, 44:46], idf[:])
        W("vector", sT)
        E("vector", "reduce_max", SC[0:2, 44:45], P3[0:2, 0:128], axis=AX.X)
        W("tensor", sV)
        E("vector", "memset", SC[0:1, 42:44], 0.0)
        W("gpsimd", sV)
        E("gpsimd", "dma_start", out=d_mx_i.ap()[0, 0:2], in_=SC[0:2, 44:45],
          inc=sGD, inck=16)
        E("gpsimd", "dma_start", out=d_mx_i.ap()[0:1, 2:4], in_=SC[0:1, 42:44],
          inc=sGD, inck=16)
        W("gpsimd", sGD)
        CC("AllReduce", ALU.max, [d_mx_i.ap()], [d_mx_o.ap()], sCC)
        W("sync", sCC)
        E("sync", "dma_start", out=SC[0:1, 16:20], in_=d_mx_o.ap(), inc=sLD, inck=16)
        W("vector", sLD)
        E("vector", "reciprocal", SC[0:1, 20:21], SC[0:1, 16:17])
        E("vector", "reciprocal", SC[0:1, 21:22], SC[0:1, 17:18])
        E("vector", "tensor_scalar", SC[0:1, 24:25], SC[0:1, 20:21], 127.0, None,
          op0=ALU.mult)
        E("vector", "tensor_scalar", SC[0:1, 25:26], SC[0:1, 21:22], 127.0, None,
          op0=ALU.mult)
        E("vector", "tensor_scalar", SC[0:1, 26:27], SC[0:1, 16:17], 1.0 / 127.0,
          None, op0=ALU.mult)
        E("vector", "tensor_scalar", SC[0:1, 27:28], SC[0:1, 17:18], 1.0 / 127.0,
          None, op0=ALU.mult)
        E("vector", "tensor_scalar", SC[0:1, 28:29], SC[0:1, 20:21], 254.0, None,
          op0=ALU.mult)
        E("vector", "tensor_scalar", SC[0:1, 29:30], SC[0:1, 21:22], 254.0, None,
          op0=ALU.mult)
        W("tensor", sV)
        E("tensor", "matmul", P3[0:128, 0:6], ones1[0:1, :], SC[0:1, 24:30],
          start=True, stop=True)
        W("vector", sT)
        E("vector", "tensor_copy", SC[:, 32:38], P3[0:128, 0:6])
        W("tensor", sV)

        rts_guard = {0: 0, 1: 0}

        def quant_slice(base_off, npieces, inv_col, d_tgt, tile_pos):
            pc = 512
            for pi in range(npieces):
                off = base_off + pi * pc
                wap = Cf[:, off:off + pc]
                i32v = SF[:].bitcast(I32)[:, 0:pc]
                cfv = SF[:, 512:1024]
                ddv = WF[:, 0:512]
                W("vector", sT)
                E("vector", "tensor_scalar", i32v, wap, SC[:, inv_col:inv_col + 1],
                  None, op0=ALU.mult)
                E("vector", "tensor_copy", cfv, i32v)
                E("vector", "scalar_tensor_tensor", ddv, wap,
                  SC[:, inv_col:inv_col + 1], cfv, op0=ALU.mult, op1=ALU.subtract)
                E("vector", "tensor_scalar", SF[:, 0:pc], ddv, 0.5, None,
                  op0=ALU.is_ge)
                E("vector", "tensor_tensor", cfv, cfv, SF[:, 0:pc], op=ALU.add)
                E("vector", "tensor_scalar", SF[:, 0:pc], ddv, -0.5, None,
                  op0=ALU.is_le)
                E("vector", "tensor_tensor", cfv, cfv, SF[:, 0:pc], op=ALU.subtract)
                E("vector", "tensor_copy", rbq[:, 0:pc], cfv)
                W("tensor", sV)
                for tj in range(4):
                    rts = rts0 if tj % 2 == 0 else rts1
                    E("tensor", "transpose", PB3[0:128, 0:128],
                      rbq[:, 128 * tj:128 * (tj + 1)], idb[:])
                    W("vector", sT)
                    W("vector", sGD, rts_guard[tj % 2])
                    E("vector", "tensor_copy", rts[:], PB3[0:128, 0:128])
                    W("tensor", sV)
                    W("gpsimd", sV)
                    r0, c0 = tile_pos(pi, tj)
                    E("gpsimd", "dma_start",
                      out=d_tgt.ap()[r0:r0 + 128, c0:c0 + 128], in_=rts[:],
                      inc=sGD, inck=16)
                    rts_guard[tj % 2] = sGD.n

        quant_slice(0, 8, IFC, d_rf_i,
                    lambda pi, tj: (512 * (pi % 2) + 128 * tj, 128 * (pi // 2)))
        quant_slice(4096, 8, IPR, d_rp_i,
                    lambda pi, tj: (128 * tj, 128 * pi))
        W("gpsimd", sGD)
        CC("AllGather", ALU.bypass, [d_rf_i.ap()], [d_rf_o.ap()], sCC)
        CC("AllGather", ALU.bypass, [d_rp_i.ap()], [d_rp_o.ap()], sCC)

        W("sync", sCC, 1)
        for r in range(NC):
            E("sync", "dma_start", out=B[0:64, 2048 * r:2048 * (r + 1)],
              in_=d_q_o.ap()[r], inc=sLD, inck=16)
        W("vector", sV)
        E("vector", "memset", Cf[:, 0:4096], 0.0)
        E("vector", "memset", Cf[:, 4096:8192], 0.0)
        W("tensor", sLD)
        W("tensor", sV)
        W("tensor", sS)

        exp_guard = {0: 0, 1: 0, 2: 0, 3: 0}
        vsl_guard = [0]
        vsl_ready = [0]
        o_guard = {0: 0, 1: 0}
        pv_guard = {0: (sS, 0), 1: (sS, 0), 2: (sS, 0), 3: (sS, 0)}
        ev_guard = {0: 0, 1: 0}
        vnb_guard = {0: 0, 1: 0}
        vnb_ready = {0: 0, 1: 0}
        ln64_guard = {0: 0, 1: 0}
        z_guard = 0
        pv_done = 0
        for c in range(NCH):
            kl = kfm[:, 128 * c:128 * (c + 1)]
            W("scalar", sT, pv_done)
            W("scalar", sV, z_guard)
            W("vector", sT, pv_done)
            for qs in range(32):
                slot = qs % 4
                pst_, off = ((P0, 0), (P0, 512), (P1, 0), (P1, 512))[slot]
                psr = pst_[0:128, off:off + 512]
                qsl = slice(512 * qs, 512 * (qs + 1))
                if qs >= 4:
                    W("tensor", sS, exp_guard[slot])
                E("tensor", "matmul", psr, kl, B[0:64, qsl],
                  start=True, stop=True)
                W("vector", sT)
                E("vector", "reduce_max", SC[:, 96 + qs:97 + qs], psr,
                  axis=AX.X, negate=True)
                W("scalar", sV)
                E("scalar", "activation", A[:, qsl], psr, AF.Exp,
                  bias=SC[:, 96 + qs:97 + qs], accum_out=SC[:, 64 + qs:65 + qs])
                exp_guard[slot] = sS.n
            W("vector", sS)
            E("vector", "tensor_reduce", SC[:, 160:161], SC[:, 96:128],
              AX.X, ALU.min)
            E("vector", "tensor_scalar", SC[:, 128:160], SC[:, 96:128],
              SC[:, 160:161], -1.0, op0=ALU.subtract, op1=ALU.mult)
            W("scalar", sV)
            E("scalar", "activation", SC[:, 128:160], SC[:, 128:160], AF.Exp)
            W("vector", sS)
            E("vector", "tensor_tensor", SC[:, 64:96], SC[:, 64:96],
              SC[:, 128:160], op=ALU.mult)
            E("vector", "reduce_sum", SC[:, 161:162], SC[:, 64:96], axis=AX.X)
            E("vector", "reciprocal", SC[:, 162:163], SC[:, 161:162])
            E("vector", "tensor_scalar", SC[:, 128:160], SC[:, 128:160],
              SC[:, 162:163], None, op0=ALU.mult)
            E("vector", "tensor_copy", evf0[:], vtm3[:, c, :])
            z_guard = sV.n
            W("tensor", sV)
            for qg in range(16):
                for si in range(2):
                    s = 2 * qg + si
                    vb = vnb0 if s % 2 == 0 else vnb1
                    if s >= 2:
                        W("vector", sT, vnb_guard[s % 2])
                    E("vector", "tensor_scalar", evf1[:], evf0[:],
                      SC[:, 128 + s:129 + s], None, op0=ALU.mult)
                    E("vector", "tensor_copy", vb[:], evf1[:])
                W("tensor", sV)
                slot = qg % 4
                pst_, off = ((P2, 0), (P2, 512), (P3, 0), (P3, 512))[slot]
                psr = pst_[0:128, off:off + 512]
                if qg >= 4:
                    W("tensor", pv_guard[slot][0], pv_guard[slot][1])
                for i in range(8):
                    qc = 8 * qg + i
                    vb = vnb0 if (qc // 4) % 2 == 0 else vnb1
                    E("tensor", "matmul", psr[:, 64 * i:64 * (i + 1)],
                      A[:, 128 * qc:128 * (qc + 1)], vb[:],
                      start=True, stop=True)
                vnb_guard[0] = sT.n
                vnb_guard[1] = sT.n
                eng, esm = (("scalar", sS) if qg % 2 == 0 else ("vector", sV))
                tgt = WF[:, 512 * (qg % 2):512 + 512 * (qg % 2)]
                W(eng, sT)
                W(eng, sG, ev_guard[qg % 2])
                if qg % 2 == 0:
                    E("scalar", "activation", tgt, psr, AF.Copy)
                else:
                    E("vector", "tensor_copy", tgt, psr)
                pv_guard[slot] = (esm, esm.n)
                W("gpsimd", esm)
                E("gpsimd", "tensor_tensor", Cf[:, 512 * qg:512 * (qg + 1)],
                  Cf[:, 512 * qg:512 * (qg + 1)], tgt, op=ALU.add)
                ev_guard[qg % 2] = sG.n
            pv_done = sT.n

        W("gpsimd", sG)
        o0v = d_o0_i.ap().rearrange("(a p) e -> p a e", p=128)
        civ = Cf[:, 0:8192].rearrange("p (a e) -> p a e", e=64)
        for g in range(8):
            E("gpsimd", "dma_start", out=o0v[:, 16 * g:16 * (g + 1), :],
              in_=civ[:, 16 * g:16 * (g + 1), :], inc=sGD, inck=16)
        W("gpsimd", sGD)
        CC("ReduceScatter", ALU.add, [d_o0_i.ap()], [d_o0_o.ap()], sCC)
        W("sync", sCC)
        E("sync", "dma_start",
          out=SF[:, 0:1024].rearrange("p (a e) -> p a e", e=64),
          in_=d_o0_o.ap().rearrange("(a p) e -> p a e", p=128),
          inc=sLD, inck=16)
        W("vector", sLD)
        W("scalar", sLD)
        W("vector", sG)
        W("scalar", sG)
        for c in range(NCH):
            o64 = SF[:, 64 * c:64 * (c + 1)]
            E("vector", "reduce_sum", mv[:, 0:1], o64, axis=AX.X)
            E("vector", "tensor_scalar", mv[:, 0:1], mv[:, 0:1], 1.0 / 64.0, None,
              op0=ALU.mult)
            ev = evf0 if c % 2 == 0 else evf1
            if c >= 2:
                W("vector", sG, ln64_guard[c % 2])
            E("vector", "tensor_scalar", ev[:], o64, mv[:, 0:1], None,
              op0=ALU.subtract)
            W("scalar", sV)
            E("scalar", "activation", SF[:, 1024:1088], ev[:], AF.Square,
              accum_out=mv[:, 1:2])
            E("scalar", "activation", sdt[:], mv[:, 1:2], AF.Sqrt, scale=1.0 / 64.0,
              bias=epsv[:])
            W("vector", sS)
            E("vector", "reciprocal", rstd[:], sdt[:])
            E("vector", "tensor_scalar", ev[:], ev[:], rstd[:, 0:1], None,
              op0=ALU.mult)
            W("gpsimd", sV)
            E("gpsimd", "tensor_tensor",
              xres[:, c, :].rearrange("p (a b) -> p a b", b=64),
              xres[:, c, :].rearrange("p (a b) -> p a b", b=64),
              ev[:, None, 0:64].broadcast_to([128, 16, 64]), op=ALU.add)
            ln64_guard[c % 2] = sG.n


        if STAGE >= 2:
            P.bar()
            u_ap = ST[0:16, 0:1024].bitcast(BF16)
            u2_ap = ST[0:16, 1024:2048].bitcast(BF16)
            bprT_c = VQ[0:16, 0:1024]
            bprbc = SF[:, 1024:2048]
            wsA3 = WS[:, 0:512].rearrange("p (a b) -> p a b", a=4)
            wsB3 = WS[:, 512:1024].rearrange("p (a b) -> p a b", a=4)
            wsRp0 = WS[:, 1024:2048]
            wsRp1 = WS[:, 2048:3072]

            W("sync", sCC, 3)
            W("sync", sGD)
            for r in range(NC):
                rfv = d_rf_o.ap()[r].rearrange("(b p) f -> p b f", p=128)
                E("sync", "dma_start", out=C3h[:, 0:4, 512 * r:512 * (r + 1)],
                  in_=rfv[:, 0:4, :], inc=sLD, inck=16)
            W("sync", sCC, 4)

            for c in range(NCH):
                ln_chunk_to_fm(c, A3, xres[:, c, :])

            W("sync", sV)
            E("sync", "dma_start", out=WF[0:16, 0:1024], in_=afc_in, inc=sLD, inck=16)
            W("vector", sLD)
            W("vector", sT, tmb_guard[0])
            E("vector", "tensor_copy", tmb[0:16, :], WF[0:16, 0:1024])
            W("tensor", sV)
            for j in range(DB):
                E("tensor", "transpose", PB3[0:128, 0:16],
                  tmb[0:16, 128 * j:128 * (j + 1)], idb[0:16, 0:16])
                W("vector", sT)
                E("vector", "tensor_copy", afcT[:, 16 * j:16 * (j + 1)],
                  PB3[0:128, 0:16])
                W("tensor", sV)
            if DEBUG:
                W("sync", sV)
                E("sync", "dma_start", out=dbg_i, in_=afcT[:], inc=sSTR, inck=16)
            W("sync", sV)
            W("sync", sT)
            E("sync", "dma_start", out=WF[0:32, 0:128], in_=bfc_in, inc=sLD, inck=16)
            W("tensor", sLD)
            E("tensor", "transpose", P3[0:128, 0:32], WF[0:32, 0:128],
              idf[0:32, 0:32])
            W("vector", sT)
            E("vector", "tensor_copy", bfcv[:], P3[0:128, 0:32])
            W("tensor", sV)
            W("sync", sT)
            E("sync", "dma_start", out=WF[0:1, 0:1024], in_=bpr_in, inc=sLD, inck=16)
            W("tensor", sLD)
            E("tensor", "matmul", P2[0:128, 0:512], ones1[0:1, :],
              WF[0:1, 0:512], start=True, stop=True)
            E("tensor", "matmul", P2[0:128, 512:1024], ones1[0:1, :],
              WF[0:1, 512:1024], start=True, stop=True)
            W("vector", sT)
            W("vector", sS)
            E("vector", "tensor_copy", bprbc[:], P2[0:128, 0:1024])
            W("tensor", sV)
            W("sync", sV)
            W("sync", sT)
            E("sync", "dma_start",
              out=WF[:, 0:512].rearrange("p (a r) -> p a r", a=32),
              in_=bfcL_in.rearrange("(a p) r -> p a r", p=128), inc=sLD, inck=16)
            W("vector", sLD)
            E("vector", "tensor_scalar", WF[:, 0:512], WF[:, 0:512],
              SC[:, TFC:TFC + 1], None, op0=ALU.mult)
            E("vector", "tensor_copy", bfcb[:], WF[:, 0:512])
            for qtr in range(4):
                W("sync", sV)
                W("sync", sT)
                E("sync", "dma_start", out=WF[0:16, 0:1024],
                  in_=apr_in[:, 1024 * qtr:1024 * (qtr + 1)], inc=sLD, inck=16)
                W("vector", sLD)
                W("vector", sT)
                E("vector", "tensor_copy", tmb[0:16, :], WF[0:16, 0:1024])
                W("tensor", sV)
                for j in range(DB):
                    fb = 8 * qtr + j
                    E("tensor", "transpose", PB3[0:128, 0:16],
                      tmb[0:16, 128 * j:128 * (j + 1)], idb[0:16, 0:16])
                    W("vector", sT)
                    E("vector", "tensor_copy", aprT[:, 16 * fb:16 * (fb + 1)],
                      PB3[0:128, 0:16])
                    W("tensor", sV)
            W("sync", sV)
            W("sync", sT)
            E("sync", "dma_start",
              out=WF[:, 0:128].rearrange("p (a r) -> p a r", a=8),
              in_=bprL_in.rearrange("(a p) r -> p a r", p=128), inc=sLD, inck=16)
            W("vector", sLD)
            E("vector", "tensor_scalar", WF[:, 0:128], WF[:, 0:128],
              SC[:, TPR:TPR + 1], None, op0=ALU.mult)
            W("vector", sT, tmb_guard[0])
            E("vector", "tensor_copy", tmb[:, 0:128], WF[:, 0:128])
            W("tensor", sV)
            for j in range(DB):
                E("tensor", "transpose", PB3[0:16, 0:128],
                  tmb[:, 16 * j:16 * (j + 1)], idb[:])
                W("vector", sT)
                E("vector", "tensor_copy", bprT_c[:, 128 * j:128 * (j + 1)],
                  PB3[0:16, 0:128])
                W("tensor", sV)
            tmb_guard[0] = sT.n

            for ts4 in range(4):
                tsl = slice(512 * ts4, 512 * (ts4 + 1))
                for j in range(DB):
                    E("tensor", "matmul", P1[0:16, 0:512],
                      afcT[:, 16 * j:16 * (j + 1)], A3[:, j, tsl],
                      start=(j == 0), stop=(j == DB - 1))
                W("vector", sT)
                E("vector", "tensor_copy", u_ap[:, tsl], P1[0:16, 0:512])
                W("tensor", sV)

            rf_guard = {0: 0, 1: 0}
            rp_guard = {0: 0, 1: 0}
            bfct_guard = {0: 0, 1: 0}
            h1_guard = {0: 0, 1: 0}
            for ts4 in range(4):
                tsl = slice(512 * ts4, 512 * (ts4 + 1))
                W("tensor", sV)
                for fb in range(32):
                    r = fb // 4
                    fs = fb % 4
                    ws = wsA3 if fb % 2 == 0 else wsB3
                    rfv = d_rf_o.ap()[r].rearrange("(b p) f -> p b f", p=128)
                    W("sync", sT, rf_guard[fb % 2])
                    E("sync", "dma_start", out=ws[:, :, :],
                      in_=rfv[:, 4:8, 128 * fs:128 * (fs + 1)], inc=sLD, inck=16)
                    bt_ = bfct0 if fb % 2 == 0 else bfct1
                    W("tensor", sV, bfct_guard[fb % 2])
                    E("tensor", "transpose", PB2[0:16, 1536:1664],
                      bfcb[:, 16 * fb:16 * (fb + 1)], idb[:])
                    W("vector", sT)
                    E("vector", "tensor_copy", bt_, PB2[0:16, 1536:1664])
                    bfct_guard[fb % 2] = sV.n
                    W("tensor", sV)
                    pp = P0 if fb % 2 == 0 else P1
                    ppg = h1_guard[fb % 2]
                    W("tensor", sS, ppg)
                    E("tensor", "matmul", pp[0:16, 0:512]
                      if False else pp[0:128, 0:512],
                      bt_, u_ap[:, tsl], start=True, stop=False)
                    for j in range(4):
                        E("tensor", "matmul", pp[0:128, 0:512],
                          C3h[:, j, 128 * fb:128 * (fb + 1)], A3[:, j, tsl],
                          start=False, stop=False)
                    W("tensor", sLD)
                    for j in range(4):
                        E("tensor", "matmul", pp[0:128, 0:512],
                          ws[:, j, :], A3[:, 4 + j, tsl],
                          start=False, stop=(j == 3))
                    rf_guard[fb % 2] = sT.n
                    W("scalar", sT)
                    E("scalar", "activation", B[:, 512 * fb:512 * (fb + 1)],
                      pp[0:128, 0:512], AF.Gelu, scale=SC[:, SFC:SFC + 1],
                      bias=bfcv[:, fb:fb + 1])
                    h1_guard[fb % 2] = sS.n
                W("tensor", sS)
                for fb in range(32):
                    E("tensor", "matmul", P2[0:16, 0:512],
                      aprT[:, 16 * fb:16 * (fb + 1)],
                      B[:, 512 * fb:512 * (fb + 1)], start=(fb == 0),
                      stop=(fb == 31))
                W("vector", sT)
                E("vector", "tensor_copy", u2_ap[:, tsl], P2[0:16, 0:512])
                W("tensor", sV)
                PQ = [P0, P1, P2, P3]
                for tsub in range(4):
                    t0 = 512 * ts4 + 128 * tsub
                    E("tensor", "matmul", PQ[tsub][0:128, 0:512],
                      u2_ap[:, t0 - 512 * ts4 + 512 * ts4:t0 + 128]
                      if False else u2_ap[:, t0:t0 + 128] if False
                      else u2_ap[:, 512 * ts4 + 128 * tsub:512 * ts4 + 128 * (tsub + 1)],
                      bprT_c[:, 0:512], start=True, stop=False)
                    E("tensor", "matmul", PQ[tsub][0:128, 512:1024],
                      u2_ap[:, 512 * ts4 + 128 * tsub:512 * ts4 + 128 * (tsub + 1)],
                      bprT_c[:, 512:1024], start=True, stop=False)
                for fb in range(32):
                    r = fb // 4
                    fs = fb % 4
                    wsp = wsRp0 if fb % 2 == 0 else wsRp1
                    W("sync", sT, rp_guard[fb % 2])
                    E("sync", "dma_start", out=wsp,
                      in_=d_rp_o.ap()[r, 128 * fs:128 * (fs + 1), :],
                      inc=sLD, inck=16)
                    W("tensor", sLD)
                    for tsub in range(4):
                        h1t = B[:, 512 * fb + 128 * tsub:512 * fb + 128 * (tsub + 1)]
                        E("tensor", "matmul", PQ[tsub][0:128, 0:512], h1t,
                          wsp[:, 0:512], start=False, stop=False)
                        E("tensor", "matmul", PQ[tsub][0:128, 512:1024], h1t,
                          wsp[:, 512:1024], start=False,
                          stop=(fb == 31))
                    rp_guard[fb % 2] = sT.n
                for tsub in range(4):
                    c = 4 * ts4 + tsub
                    W("vector", sT)
                    E("vector", "tensor_scalar", SF[:, 0:1024],
                      PQ[tsub][0:128, 0:1024], SC[:, SPR:SPR + 1], None,
                      op0=ALU.mult)
                    E("vector", "tensor_tensor", SF[:, 0:1024], SF[:, 0:1024],
                      bprbc[:], op=ALU.add)
                    post_ln_res(c, SF[:, 0:1024], WF[:, 0:1024])
                h1_guard = {0: sS.n, 1: sS.n}

        if STAGE >= 3:
            P.bar()
            wcb = WS[:, 0:1024].bitcast(F32)
            STB = WS[:, 1024:2048]
            wqbuf = WS[:, 2048:4096]
            wqb3 = wqbuf.rearrange("p (a j) -> p a j", a=2)
            vqb1 = SFB[:, 2048:3072]
            vq3a = VQ[:].rearrange("p (a t) -> p a t", a=8)
            vq3b = vqb1.rearrange("p (a t) -> p a t", a=8)

            for c in range(NCH):
                ln_chunk_to_fm(c, A3, xres[:, c, :])

            W("sync", sV)
            E("sync", "dma_start",
              out=WF[:, 0:640].rearrange("p (a g) -> p a g", a=DB),
              in_=gw_in.rearrange("(a p) g -> p a g", p=128), inc=sLD, inck=16)
            W("vector", sLD)
            E("vector", "tensor_copy", gwb[:], WF[:, 0:640])
            E("sync", "dma_start", out=WF[0:1, 640:672], in_=gb_in,
              inc=sLD, inck=16)
            E("sync", "dma_start", out=WF[0:1, 672:680], in_=mask_in,
              inc=sLD, inck=16)
            W("tensor", sLD)
            W("tensor", sV)
            E("tensor", "matmul", P3[0:128, 0:32], ones1[0:1, :],
              WF[0:1, 640:672], start=True, stop=True)
            E("tensor", "matmul", P3[0:128, 32:40], ones1[0:1, :],
              WF[0:1, 672:680], start=True, stop=True)
            W("vector", sT)
            E("vector", "memset", gbias[:], 0.0)
            E("vector", "tensor_copy", gbias[:, 16:48], P3[0:128, 0:32])
            E("vector", "tensor_copy", maskb[:], P3[0:128, 32:40])
            E("vector", "memset", wprod[:], 1.0)
            E("vector", "memset", SC[:, 40:41], -5.0)
            W("tensor", sV)

            PK = [P0, P1]; PV_ = [P2, P3]
            kv_guard = [0]
            for g in range(8):
                cpair = (2 * g, 2 * g + 1)
                for wi, (w_in_, pp) in enumerate(((k1, PK), (v1, PV_))):
                    for db in range(8):
                        W("sync", sT, kv_guard[0])
                        W("sync", sV)
                        E("sync", "dma_start", out=WF[:, 0:1024],
                          in_=w_in_[128 * db:128 * (db + 1), :], inc=sLD, inck=16)
                        W("vector", sLD)
                        E("vector", "tensor_copy", wqb3[:, db % 2, :],
                          WF[:, 0:1024])
                        if False:
                            pass
                            E("sync", "dma_start", out=dbg_l[:, 0:512]
                              if False else dbg_l[0:128, 0:512],
                              in_=WF[:, 0:512], inc=sSTR, inck=16)
                            W("tensor", sSTR, sSTR.n)
                            W("vector", sSTR, sSTR.n)
                        W("tensor", sV)
                        for ci, c in enumerate(cpair):
                            for jj in range(2):
                                E("tensor", "matmul",
                                  pp[ci][0:128, 512 * jj:512 * (jj + 1)],
                                  A3[:, db, 128 * c:128 * (c + 1)],
                                  wqb3[:, db % 2, 512 * jj:512 * (jj + 1)],
                                  start=(db == 0), stop=(db == 7),
                                  skip_group_check=True)
                        kv_guard[0] = sT.n
                for ci, c in enumerate(cpair):
                    kf = SF[:, 0:1024]
                    vf = SF[:, 1024:2048]
                    kf3 = kf.rearrange("p (h e) -> p h e", h=16)
                    vf3 = vf.rearrange("p (h e) -> p h e", h=16)
                    W("scalar", sT)
                    W("scalar", sV)
                    W("scalar", sG)
                    if DEBUG and g == 0 and ci == 0:
                        W("vector", sT)
                        E("vector", "tensor_copy", evf0[:], PK[0][0:128, 0:64])
                        W("sync", sV)
                        E("sync", "dma_start", out=dbg_f[:, 0:64], in_=evf0[:],
                          inc=sSTR, inck=16)
                        W("scalar", sSTR, sSTR.n)
                        W("vector", sSTR, sSTR.n)
                    E("scalar", "activation", kf, PK[ci][0:128, 0:1024], AF.Silu)
                    E("scalar", "activation", vf, PV_[ci][0:128, 0:1024], AF.Silu)
                    if DEBUG and g == 0 and ci == 0:
                        W("sync", sS)
                        E("sync", "dma_start", out=dbg_l[:, 512:1024],
                          in_=SF[:, 0:512], inc=sSTR, inck=16)
                        W("vector", sSTR, sSTR.n)
                    W("tensor", sS)
                    for db in range(DB):
                        E("tensor", "matmul", PK[ci][0:128, 0:80],
                          A3[:, db, 128 * c:128 * (c + 1)], gwb3[:, db, :],
                          start=(db == 0), stop=(db == DB - 1))
                    W("vector", sT)
                    gt = SC[:, 96:176]
                    E("vector", "tensor_tensor", gt, PK[ci][0:128, 0:80],
                      gbias[:], op=ALU.add)
                    W("scalar", sV)
                    E("scalar", "activation", gt, gt, AF.Sigmoid)
                    W("vector", sS)
                    E("vector", "tensor_tensor", kf3, kf3,
                      SC[:, None, 96:112].rearrange("p a h -> p h a")
                      .broadcast_to([128, 16, 64]), op=ALU.mult)
                    E("vector", "tensor_tensor", vf3, vf3,
                      SC[:, None, 96:112].rearrange("p a h -> p h a")
                      .broadcast_to([128, 16, 64]), op=ALU.mult)
                    W("gpsimd", sV)
                    E("gpsimd", "tensor_copy", B[:, 1024 * c:1024 * (c + 1)], kf)
                    E("gpsimd", "tensor_copy", C[:, 1024 * c:1024 * (c + 1)], vf)
                    wf3 = WF[:, 0:1024].rearrange("p (h e) -> p h e", h=16)
                    E("vector", "tensor_tensor", wf3, kf3, vf3, op=ALU.mult)
                    E("vector", "reduce_sum",
                      SC[:, 176:192].rearrange("p (a b) -> p a b", b=1), wf3,
                      axis=AX.X)
                    E("vector", "tensor_tensor", wf3, kf3, kf3, op=ALU.mult)
                    E("vector", "reduce_sum",
                      SC[:, 0:16].rearrange("p (a b) -> p a b", b=1), wf3,
                      axis=AX.X)
                    E("vector", "tensor_tensor", wf3, vf3, vf3, op=ALU.mult)
                    E("vector", "reduce_sum",
                      SC[:, 16:32].rearrange("p (a b) -> p a b", b=1), wf3,
                      axis=AX.X)
                    W("scalar", sV)
                    E("scalar", "activation", SC[:, 0:16], SC[:, 0:16], AF.Sqrt)
                    E("scalar", "activation", SC[:, 16:32], SC[:, 16:32], AF.Sqrt)
                    W("vector", sS)
                    E("vector", "tensor_scalar", SC[:, 0:16], SC[:, 0:16], 1e-8,
                      None, op0=ALU.add)
                    E("vector", "tensor_scalar", SC[:, 16:32], SC[:, 16:32], 1e-8,
                      None, op0=ALU.add)
                    E("vector", "tensor_tensor", SC[:, 0:16], SC[:, 0:16],
                      SC[:, 16:32], op=ALU.mult)
                    E("vector", "reciprocal", SC[:, 16:32], SC[:, 0:16])
                    W("scalar", sV)
                    E("scalar", "activation", SC[:, 0:16], SC[:, 176:192], AF.Abs)
                    W("vector", sS)
                    E("vector", "tensor_tensor", SC[:, 0:16], SC[:, 0:16],
                      SC[:, 16:32], op=ALU.mult)
                    W("scalar", sV)
                    E("scalar", "activation", SC[:, 16:32], SC[:, 0:16],
                      AF.Sigmoid, scale=3.0)
                    W("scalar", sS)
                    E("scalar", "activation", SC[:, 0:16], SC[:, 16:32],
                      AF.Sigmoid, scale=10.0, bias=SC[:, 40:41])
                    W("vector", sS)
                    E("vector", "tensor_mul", s2st[:, 16 * c:16 * (c + 1)],
                      SC[:, 0:16], SC[:, 0:16])
                    E("vector", "tensor_scalar", SC[:, 48:64], SC[:, 16:32], 0.3,
                      0.5, op0=ALU.mult, op1=ALU.add)
                    E("vector", "tensor_tensor", afg[:, 16 * c:16 * (c + 1)],
                      SC[:, 48:64], gt[:, 48:64], op=ALU.mult)
                    E("vector", "tensor_scalar", SC[:, 48:64], SC[:, 16:32], -0.3,
                      0.5, op0=ALU.mult, op1=ALU.add)
                    E("vector", "tensor_tensor", asg[:, 16 * c:16 * (c + 1)],
                      SC[:, 48:64], gt[:, 64:80], op=ALU.mult)
                    E("vector", "tensor_scalar", SC[:, 48:64], SC[:, 16:32], -0.1,
                      1.0, op0=ALU.mult, op1=ALU.add)
                    E("vector", "tensor_tensor", SC[:, 48:64], SC[:, 48:64],
                      gt[:, 16:32], op=ALU.mult)
                    E("vector", "tensor_scalar", SC[:, 64:80], SC[:, 16:32], -0.05,
                      1.0, op0=ALU.mult, op1=ALU.add)
                    E("vector", "tensor_tensor", SC[:, 64:80], SC[:, 64:80],
                      gt[:, 32:48], op=ALU.mult)
                    W("tensor", sV)
                    E("tensor", "matmul", PK[ci][0:1, 0:32], onesf[:, 0:1],
                      SC[:, 48:80], start=True, stop=True)
                    W("vector", sT)
                    E("vector", "tensor_scalar", wm[0:1, 32 * c:32 * (c + 1)],
                      PK[ci][0:1, 0:32], 1.0 / 128.0, None, op0=ALU.mult)

            for _ in range(7):
                E("vector", "tensor_mul", wm[:], wm[:], wm[:])
            for c in range(NCH):
                E("vector", "tensor_mul", wprod[:], wprod[:],
                  wm[0:1, 32 * c:32 * (c + 1)])
            W("tensor", sV)
            E("tensor", "matmul", P3[0:128, 0:512], ones1[0:1, :], wm[:],
              start=True, stop=True)
            E("tensor", "matmul", P3[0:128, 512:544], ones1[0:1, :], wprod[:],
              start=True, stop=True)
            W("vector", sT)
            E("vector", "tensor_copy", wcb[:], P3[0:128, 0:512])
            E("vector", "tensor_copy", WF[:, 0:32], P3[0:128, 512:544])
            W("tensor", sV)

            W("tensor", sS)
            E("vector", "memset", stateF[:], 0.0)
            vk_guard = {0: 0, 1: 0}
            def vk_update(c):
                W("vector", sT)
                W("vector", sG)
                E("vector", "tensor_tensor",
                  WF[:, 0:1024].rearrange("p (h e) -> p h e", h=16),
                  C3v[:, c, :, :],
                  s2st[:, 16 * c:16 * (c + 1)]
                  .rearrange("p (h a) -> p h a", a=1)
                  .broadcast_to([128, 16, 64]), op=ALU.mult)
                E("vector", "tensor_copy", tmb[:], WF[:, 0:1024])
                W("tensor", sV, vk_guard[0])
                tm3 = tmb[:].rearrange("p (h e) -> p h e", h=16)
                for r2 in range(2):
                    for h in range(16):
                        par = h % 2
                        pvk = P2 if par == 0 else P3
                        reg = pvk[64 * par:64 * par + 64,
                                  128 * (h // 2) + 64 * r2:
                                  128 * (h // 2) + 64 * r2 + 64]
                        lhs = C3v[:, c, h, :] if r2 == 0 else tm3[:, h, :]
                        E("tensor", "matmul", reg, lhs, B3v[:, c, h, :],
                          start=True, stop=True)
                W("vector", sT)
                for par in range(2):
                    rsl = slice(64 * par, 64 * par + 64)
                    pvk = P2 if par == 0 else P3
                    dview = (wcb[rsl, 32 * c + par:32 * c + 32]
                             .rearrange("p (r h) -> p r h", r=2)[:, :, 0:16:2]
                             if False else
                             wcb[rsl, 32 * c:32 * (c + 1)]
                             .rearrange("p (r q) -> p r q", r=2)[:, :, par:16:2])
                    dv = dview.rearrange("p r h -> p h r")[:, :, :, None]                         .broadcast_to([64, 8, 2, 64])
                    sview = stateF[rsl, :].rearrange("p (h r e) -> p h r e",
                                                     h=8, r=2)
                    E("vector", "tensor_tensor", sview, sview, dv, op=ALU.mult)
                    E("vector", "tensor_tensor", stateF[rsl, :], stateF[rsl, :],
                      pvk[rsl, 0:1024], op=ALU.add)
                vk_guard[0] = sV.n

            for c in range(NCH):
                vk_update(c)
            W("gpsimd", sV)
            E("gpsimd", "dma_start", out=d_st_i.ap()[:, 0:1024], in_=stateF[:],
              inc=sGD, inck=16)
            E("gpsimd", "dma_start", out=d_st_i.ap()[:, 1024:1056],
              in_=WF[:, 0:32], inc=sGD, inck=16)
            W("gpsimd", sGD)
            CC("AllGather", ALU.bypass, [d_st_i.ap()], [d_st_o.ap()], sCC)

            selbuf = SF[:, 1024:2048]
            E("vector", "memset", selbuf, 0.0)
            E("vector", "memset", stateF[:], 0.0)
            W("sync", sCC)
            W("sync", sGD)
            for rk in range(7):
                W("sync", sV)
                E("sync", "dma_start", out=WF[:, 0:1024],
                  in_=d_st_o.ap()[rk, :, 0:1024], inc=sLD, inck=16)
                E("sync", "dma_start", out=SC[:, 96:128],
                  in_=d_st_o.ap()[rk, :, 1024:1056], inc=sLD, inck=16)
                W("vector", sLD)
                for idx in range(32):
                    hp, r2x = idx // 4, (idx % 4) // 2
                    for par in range(2):
                        rsl = slice(64 * par, 64 * par + 64)
                        sl = stateF[rsl, 32 * idx:32 * (idx + 1)]
                        pwc = 96 + 16 * r2x + 2 * hp + par
                        E("vector", "scalar_tensor_tensor", sl, sl,
                          SC[rsl, pwc:pwc + 1],
                          WF[rsl, 32 * idx:32 * (idx + 1)],
                          op0=ALU.mult, op1=ALU.add)
                E("vector", "scalar_tensor_tensor", selbuf, stateF[:],
                  maskb[:, rk:rk + 1], selbuf, op0=ALU.mult, op1=ALU.add)
            E("vector", "tensor_copy", stateF[:], selbuf)

            vk_guard = {0: 0, 1: 0}
            q_guard = [0]
            for cp in range(8):
                c0, c1 = 2 * cp, 2 * cp + 1
                for db in range(8):
                    W("sync", sT, q_guard[0])
                    W("sync", sV)
                    E("sync", "dma_start", out=WF[:, 0:1024],
                      in_=q1[128 * db:128 * (db + 1), :], inc=sLD, inck=16)
                    W("vector", sLD)
                    E("vector", "tensor_copy", wqb3[:, db % 2, :], WF[:, 0:1024])
                    W("tensor", sV)
                    W("tensor", sS)
                    for ci, c in enumerate((c0, c1)):
                        pq = P0 if ci == 0 else P1
                        for jj in range(2):
                            E("tensor", "matmul",
                              pq[0:128, 512 * jj:512 * (jj + 1)],
                              A3[:, db, 128 * c:128 * (c + 1)],
                              wqb3[:, db % 2, 512 * jj:512 * (jj + 1)],
                              start=(db == 0), stop=(db == 7),
                              skip_group_check=True)
                    q_guard[0] = sT.n
                for ci, c in enumerate((c0, c1)):
                    vq3 = vq3a
                    pq = P0 if ci == 0 else P1
                    W("scalar", sT)
                    W("scalar", sV)
                    E("scalar", "activation", SF[:, 0:1024], pq[0:128, 0:1024],
                      AF.Silu)
                    W("vector", sS)
                    W("vector", sT, tmb_guard[0])
                    E("vector", "tensor_copy", tmb[:], SF[:, 0:1024])
                    W("tensor", sV)
                    for jb in range(DB):
                        E("tensor", "transpose",
                          PB2[0:128, 128 * jb:128 * (jb + 1)],
                          tmb[:, 128 * jb:128 * (jb + 1)], idb[:])
                    W("vector", sT)
                    E("vector", "tensor_copy",
                      VQ[:].rearrange("p (a t) -> p a t", a=DB),
                      PB2[0:128, 0:1024].rearrange("p (a t) -> p a t", a=DB))
                    tmb_guard[0] = sT.n
                    W("tensor", sV)
                    vk_update(c)
                    W("vector", sV)
                    E("vector", "tensor_copy", STB, stateF[:])
                    W("tensor", sV)
                    stb4 = STB.rearrange("p (a r e) -> p a r e", a=8, r=2)
                    for h in range(16):
                        par = h % 2
                        qh = vq3[64 * par:64 * par + 64, h // 2, :]
                        E("tensor", "matmul", P2[0:128, 64 * h:64 * (h + 1)], qh,
                          stb4[64 * par:64 * par + 64, h // 2, 0, :],
                          start=True, stop=True)
                        E("tensor", "matmul", P3[0:128, 64 * h:64 * (h + 1)], qh,
                          stb4[64 * par:64 * par + 64, h // 2, 1, :],
                          start=True, stop=True)
                    W("vector", sT)
                    afv = (afg[:, 16 * c:16 * (c + 1)]
                           .rearrange("p (h a) -> p h a", a=1)
                           .broadcast_to([128, 16, 64]))
                    asv = (asg[:, 16 * c:16 * (c + 1)]
                           .rearrange("p (h a) -> p h a", a=1)
                           .broadcast_to([128, 16, 64]))
                    p2v = P2[0:128, 0:1024].rearrange("p (h e) -> p h e", h=16)
                    p3v = P3[0:128, 0:1024].rearrange("p (h e) -> p h e", h=16)
                    sfv = SF[:, 0:1024].rearrange("p (h e) -> p h e", h=16)
                    wfv = WF[:, 0:1024].rearrange("p (h e) -> p h e", h=16)
                    E("vector", "tensor_tensor", sfv, p2v, afv, op=ALU.mult)
                    E("vector", "tensor_tensor", wfv, p3v, asv, op=ALU.mult)
                    E("vector", "tensor_tensor", SF[:, 0:1024], SF[:, 0:1024],
                      WF[:, 0:1024], op=ALU.add)
                    W("vector", sT, tmb_guard[0])
                    E("vector", "tensor_copy", tmb[:], SF[:, 0:1024])
                    chunk_to_fm(c, A3)
            PQ4 = [P0, P1, P2, P3]
            wo_guard = [0]
            for grp in range(4):
                for db in range(8):
                    W("sync", sT, wo_guard[0])
                    W("sync", sV)
                    E("sync", "dma_start", out=WF[:, 0:1024],
                      in_=o1[128 * db:128 * (db + 1), :], inc=sLD, inck=16)
                    W("vector", sLD)
                    E("vector", "tensor_copy", wqb3[:, db % 2, :], WF[:, 0:1024])
                    W("tensor", sV)
                    for ci in range(4):
                        c = 4 * grp + ci
                        for jj in range(2):
                            E("tensor", "matmul",
                              PQ4[ci][0:128, 512 * jj:512 * (jj + 1)],
                              A3[:, db, 128 * c:128 * (c + 1)],
                              wqb3[:, db % 2, 512 * jj:512 * (jj + 1)],
                              start=(db == 0), stop=(db == 7),
                              skip_group_check=True)
                    wo_guard[0] = sT.n
                for ci in range(4):
                    c = 4 * grp + ci
                    W("vector", sT)
                    W("vector", sS)
                    E("vector", "tensor_copy", SF[:, 0:1024],
                      PQ4[ci][0:128, 0:1024])
                    post_ln_res(c, SF[:, 0:1024], SF[:, 1024:2048])

        if STAGE >= 4:
            P.bar()
            yn_sb = Cf[0:1, 0:8192]
            ynb = Cf[0:64, 4096:5120]
            bexp_sb = Cf[0:64, 5120:6144]
            ynbf = C[0:64, 12288:13312]
            B3p = B[:].rearrange("p (a t) -> p a t", a=DB)
            for c in range(NCH):
                ln_chunk_to_fm(c, A3, xres[:, c, :])
            for db in range(DB):
                W("scalar", sV)
                E("scalar", "activation", C[:, 0:2048], A3[:, db, :], AF.Copy,
                  accum_out=Sfm[:, db:db + 1])
            W("gpsimd", sS)
            E("gpsimd", "dma_start", out=d_S_i.ap(), in_=Sfm[:], inc=sGD, inck=16)
            W("gpsimd", sGD)
            CC("AllReduce", ALU.add, [d_S_i.ap()], [d_S_o.ap()], sCC)
            W("sync", sCC)
            E("sync", "dma_start", out=Sfm[:], in_=d_S_o.ap(), inc=sLD, inck=16)
            W("vector", sLD)
            E("vector", "tensor_copy", Sbf[:], Sfm[:])
            yn_guard = [0]
            for q4 in range(4):
                for db in range(DB):
                    for half in range(2):
                        W("sync", sT, yn_guard[0])
                        W("sync", sV)
                        E("sync", "dma_start", out=WF[:, 0:1024],
                          in_=wexp_in.rearrange("n (a p) e -> n p a e", p=128)[
                              2 * q4 + half, :, db, :], inc=sLD, inck=16)
                        W("vector", sLD)
                        E("vector", "tensor_copy",
                          wqb3[:, half, :], WF[:, 0:1024])
                    W("tensor", sV)
                    W("tensor", sS)
                    pq_regions = [(P0, 0), (P0, 512), (P1, 0), (P1, 512)]
                    for j in range(4):
                        pt, c0_ = pq_regions[j]
                        E("tensor", "matmul", pt[0:1, c0_:c0_ + 512],
                          Sbf[:, db:db + 1],
                          wqbuf[:, 512 * j:512 * (j + 1)],
                          start=(db == 0), stop=(db == DB - 1),
                          skip_group_check=True)
                    yn_guard[0] = sT.n
                W("vector", sT)
                for j in range(4):
                    pt, c0_ = [(P0, 0), (P0, 512), (P1, 0), (P1, 512)][j]
                    g = 2048 * q4 + 512 * j
                    E("vector", "tensor_copy", yn_sb[0:1, g:g + 512],
                      pt[0:1, c0_:c0_ + 512])
            W("gpsimd", sV)
            E("gpsimd", "dma_start", out=d_yn_i.ap().rearrange("a b -> (a b)"),
              in_=yn_sb, inc=sGD, inck=16)
            W("gpsimd", sGD)
            CC("AllGather", ALU.bypass, [d_yn_i.ap()], [d_yn_o.ap()], sCC)
            W("sync", sCC)
            for r in range(NC):
                E("sync", "dma_start", out=ynb[8 * r:8 * (r + 1), :],
                  in_=d_yn_o.ap()[r].rearrange("a (n e) -> (a n) e", e=1024),
                  inc=sLD, inck=16)
            E("sync", "dma_start", out=bexp_sb, in_=bexp_in, inc=sLD, inck=16)
            W("vector", sLD)
            E("vector", "tensor_tensor", ynb, ynb, bexp_sb, op=ALU.add)
            E("vector", "tensor_copy", ynbf, ynb)
            for db in range(DB):
                for half in range(2):
                    W("sync", sV)
                    E("sync", "dma_start", out=WF[:, 0:1024],
                      in_=phi_in[128 * db:128 * (db + 1),
                                 1024 * half:1024 * (half + 1)],
                      inc=sLD, inck=16)
                    W("vector", sLD)
                    E("vector", "tensor_copy",
                      B3p[:, db, 1024 * half:1024 * (half + 1)], WF[:, 0:1024])
            W("tensor", sV)
            lg_guard = [0]
            for c in range(NCH):
                csl = slice(128 * c, 128 * (c + 1))
                W("tensor", sV, lg_guard[0])
                for db in range(DB):
                    for j in range(4):
                        pt = P2 if j < 2 else P3
                        c0_ = 512 * (j % 2)
                        E("tensor", "matmul", pt[0:128, c0_:c0_ + 512],
                          A3[:, db, csl], B3p[:, db, 512 * j:512 * (j + 1)],
                          start=(db == 0), stop=(db == DB - 1),
                          skip_group_check=True)
                W("vector", sT)
                W("vector", sG)
                E("vector", "tensor_copy", SF[:, 0:1024], P2[0:128, 0:1024])
                E("vector", "tensor_copy", SF[:, 1024:2048], P3[0:128, 0:1024])
                E("vector", "reduce_max", SC[:, 64:65], SF[:, 0:2048], axis=AX.X)
                E("vector", "tensor_scalar", SC[:, 65:66], SC[:, 64:65], -1.0,
                  None, op0=ALU.mult)
                W("scalar", sV)
                E("scalar", "activation", Cf[:, 0:2048], SF[:, 0:2048], AF.Exp,
                  bias=SC[:, 65:66], accum_out=SC[:, 66:67])
                W("vector", sS)
                E("vector", "reciprocal", SC[:, 67:68], SC[:, 66:67])
                E("vector", "reduce_sum",
                  SC[:, 0:64].rearrange("p (a b) -> p a b", b=1),
                  Cf[:, 0:2048].rearrange("p (a b) -> p a b", b=32), axis=AX.X)
                E("vector", "tensor_scalar", SC[:, 0:64], SC[:, 0:64],
                  SC[:, 67:68], None, op0=ALU.mult)
                W("vector", sT, tmb_guard[0])
                E("vector", "tensor_copy", tmb[:, 0:64], SC[:, 0:64])
                W("tensor", sV)
                cht = chT0 if c % 2 == 0 else chT1
                E("tensor", "transpose", PB2[0:64, 1536:1664], tmb[:, 0:64],
                  idb[:])
                tmb_guard[0] = sT.n
                W("vector", sT)
                E("vector", "tensor_copy", cht, PB2[0:64, 1536:1664])
                W("tensor", sV)
                E("tensor", "matmul", P0[0:128, 0:512], cht, ynbf[:, 0:512],
                  start=True, stop=True)
                E("tensor", "matmul", P0[0:128, 512:1024], cht, ynbf[:, 512:1024],
                  start=True, stop=True)
                W("vector", sT)
                E("vector", "tensor_copy", SF[:, 0:1024], P0[0:128, 0:1024])
                lg_guard[0] = sV.n
                post_ln_res(c, SF[:, 0:1024], SF[:, 1024:2048])
        P.bar()
        W("sync", sG)
        W("sync", sV)
        W("sync", sS)
        W("sync", sT)
        E("sync", "dma_start", out=y_out.rearrange("(a p) d -> p a d", p=128),
          in_=xres[:], inc=sSTR, inck=16)
        if DEBUG:
            E("vector", "tensor_copy", WF[:, 0:512], A[:, 0:512])
            E("vector", "tensor_copy", WF[:, 512:1024],
              vtm3[:, 0:8, :].rearrange("p a b -> p (a b)"))
            W("sync", sV)
            E("sync", "dma_start", out=dbg_sc, in_=SC[:], inc=sSTR, inck=16)
            E("sync", "dma_start", out=dbg_a, in_=WF[:, 0:512], inc=sSTR, inck=16)
            E("sync", "dma_start", out=dbg_b, in_=WF[:, 512:1024], inc=sSTR, inck=16)
            E("sync", "dma_start", out=dbg_c, in_=Cf[:, 0:512], inc=sSTR, inck=16)
            if STAGE >= 3:
                E("sync", "dma_start", out=dbg_j, in_=stateF[:], inc=sSTR, inck=16)
                W("sync", sLD)
                E("vector", "tensor_copy", WF[:, 0:512], C[:, 0:512])
                E("vector", "tensor_copy", WF[:, 512:768], s2st[:])
                E("vector", "tensor_copy", WF[:, 768:1024], afg[:])
                W("sync", sV)
                E("sync", "dma_start", out=dbg_k, in_=WF[:, 0:1024],
                  inc=sSTR, inck=16)
        W("gpsimd", sSTR, sSTR.n)

        with nc.Block() as block:
            for name in ENGINES:
                def engine_fn(eng, name=name):
                    for it_name, thunk in P.items:
                        if it_name == name:
                            thunk(eng)
                getattr(block, name)(engine_fn)
    return nc


_NC_CACHE = {}
_LAST_RES = None


def kernel(**inputs):
    x = np.ascontiguousarray(np.asarray(inputs["L0_attn_x"], np.float32)[0])
    in_maps = []
    for c in range(NC):
        m = {
            "x_sh": np.ascontiguousarray(x[TPC * c:TPC * (c + 1)]),
            "wq0": np.ascontiguousarray(inputs["L0_attn_Wq"]),
            "wk0": np.ascontiguousarray(inputs["L0_attn_Wk"]),
            "wv0": np.ascontiguousarray(inputs["L0_attn_Wv"]),
            "wfc_sl": np.ascontiguousarray(
                inputs["L0_ffn_W_fc"][512 * c:512 * (c + 1)]),
            "bfc": np.ascontiguousarray(inputs["L0_ffn_b_fc"].reshape(32, 128)),
            "afc": np.ascontiguousarray(inputs["L0_ffn_A_fc"]),
            "bfcL": np.ascontiguousarray(inputs["L0_ffn_B_fc"]),
            "wpr_sl": np.ascontiguousarray(
                inputs["L0_ffn_W_proj"][:, 512 * c:512 * (c + 1)]),
            "bpr": np.ascontiguousarray(inputs["L0_ffn_b_proj"].reshape(1, D)),
            "apr": np.ascontiguousarray(inputs["L0_ffn_A_proj"]),
            "bprL": np.ascontiguousarray(inputs["L0_ffn_B_proj"]),
            "q1w": np.ascontiguousarray(inputs["L1_attn_Wq"]),
            "k1w": np.ascontiguousarray(inputs["L1_attn_Wk"]),
            "v1w": np.ascontiguousarray(inputs["L1_attn_Wv"]),
            "o1w": np.ascontiguousarray(inputs["L1_attn_Wo"]),
            "gw": np.ascontiguousarray(np.concatenate(
                [inputs["L1_attn_Wb"], inputs["L1_attn_Wfd"],
                 inputs["L1_attn_Wsd"], inputs["L1_attn_Wfg"],
                 inputs["L1_attn_Wsg"]], axis=1)),
            "gbias": np.ascontiguousarray(np.stack(
                [inputs["L1_attn_fd_bias"], inputs["L1_attn_sd_bias"]])),
            "phi": np.ascontiguousarray(inputs["L1_ffn_phi"].reshape(D, 2048)),
            "wexp": np.ascontiguousarray(inputs["L1_ffn_W"][8 * c:8 * (c + 1)]),
            "bexp": np.ascontiguousarray(inputs["L1_ffn_b"]),
            "mask": (np.eye(8, dtype=np.float32)[c - 1:c] if c > 0
                     else np.zeros((1, 8), np.float32)),
        }
        in_maps.append(m)
    if "prog" not in _NC_CACHE:
        _NC_CACHE["prog"] = build()
    res = run_bass_kernel_spmd(_NC_CACHE["prog"], in_maps, core_ids=list(range(NC)))
    global _LAST_RES
    _LAST_RES = res
    y = np.concatenate([np.asarray(r["y"], np.float32) for r in res.results], axis=0)
    return y.reshape(1, 16384, 1024)
